# version 02
# baseline (speedup 1.0000x reference)
import math
from contextlib import ExitStack

import numpy as np
import ml_dtypes
import concourse.bass as bass
import concourse.mybir as mybir
from concourse.bass_utils import run_bass_kernel_spmd

F32 = mybir.dt.float32
BF16 = mybir.dt.bfloat16
AF = mybir.ActivationFunctionType
ALU = mybir.AluOpType
AX = mybir.AxisListType

T = 2048
DM = 2048
NFT = 11
L_INIT = 0.8 - 0.6 * math.exp(-0.3 * 12)
EPS = 1e-5
BF = ml_dtypes.bfloat16

ENGINES = ("sync", "gpsimd", "tensor", "vector", "scalar")


class PB:

    def __init__(self):
        self.ops = {e: [] for e in ENGINES}
        self.K = {n: 0 for n in ("sq", "gq", "pe", "ac", "ve", "so",
                                 "vv", "aa")}

    def vsem(self, build):
        val = self.add("vector", build, inc="vv")
        self.wait("vector", "vv", val)
        return val

    def asem(self, build):
        val = self.add("scalar", build, inc="aa")
        self.wait("scalar", "aa", val)
        return val

    def add(self, eng, build, inc=None, amt=1):
        val = None
        if inc is not None:
            self.K[inc] += amt
            val = self.K[inc]

        def closure(sec, sems, build=build, inc=inc, amt=amt):
            ins = build(sec)
            if inc is not None:
                ins.then_inc(sems[inc], amt)
        self.ops[eng].append(closure)
        return val

    def dma(self, eng, sem, out, in_, **kw):
        return self.add(eng, lambda s: s.dma_start(out=out, in_=in_, **kw),
                        inc=sem, amt=16)

    def dmat(self, eng, sem, out, in_):
        return self.add(eng, lambda s: s.dma_start_transpose(out, in_),
                        inc=sem, amt=16)

    def wait(self, eng, sem, val):
        if not val:
            return

        def closure(sec, sems, sem=sem, val=val):
            sec.wait_ge(sems[sem], val)
        self.ops[eng].append(closure)

    def barrier_into(self, engines, conds):
        for e in engines:
            for sem, val in conds:
                self.wait(e, sem, val)

    def emit(self, block, sems):
        for eng in ENGINES:
            ops = self.ops[eng]
            if not ops:
                continue

            def body(section, ops=ops):
                for op in ops:
                    op(section, sems)
            getattr(block, eng)(body)


def host_constants():
    n_freq = T // 2 + 1
    sigma = 0.15
    xs = np.linspace(0.0, 1.0, n_freq)
    filt = np.exp(-0.5 * (xs / sigma) ** 2)
    h = np.fft.irfft(filt, n=T)
    idx = (np.arange(T)[:, None] - np.arange(T)[None, :]) % T
    C = h[idx].astype(np.float32)
    inv_freq = 1.0 / 10000.0 ** (np.arange(0, 64, 2, dtype=np.float64) / 64)
    tt = np.arange(T, dtype=np.float64)
    freqs = np.outer(tt, inv_freq)
    cos, sin = np.cos(freqs), np.sin(freqs)
    pmod = np.arange(128) % 32
    CC = np.ascontiguousarray(cos[:, pmod].T).astype(np.float32)
    sgn = np.where((np.arange(128) % 64) < 32, -1.0, 1.0)
    SS = np.ascontiguousarray((sin[:, pmod].T) * sgn[:, None]).astype(np.float32)
    perm = np.empty(2048, dtype=np.int64)
    for g in range(32):
        b0 = 64 * g
        perm[b0:b0 + 32] = b0 + 2 * np.arange(32)
        perm[b0 + 32:b0 + 64] = b0 + 2 * np.arange(32) + 1
    pp = np.arange(128)[:, None]
    col = np.arange(512)[None, :]
    masks4 = np.concatenate(
        [(col >= (128 * oi + pp)).astype(np.float32) for oi in range(4)],
        axis=1).astype(BF)
    ident = np.eye(128, dtype=np.float32).astype(BF)
    return C, CC, SS, perm, masks4, ident


def build_program():
    nc = bass.Bass("TRN2", target_bir_lowering=False, detect_race_conditions=False)
    dt = nc.dram_tensor
    x_full = dt("x_full", [T, DM], F32, kind="ExternalInput")
    x_mine = dt("x_mine", [1024, DM], F32, kind="ExternalInput")
    wq = dt("wq", [DM, DM], F32, kind="ExternalInput")
    wk = dt("wk", [DM, DM], F32, kind="ExternalInput")
    wv = dt("wv", [DM, DM], F32, kind="ExternalInput")
    wout = dt("wout", [DM, DM], F32, kind="ExternalInput")
    rmsw = dt("rmsw", [128, 1], F32, kind="ExternalInput")
    lam_in = dt("lam_in", [1, 256], F32, kind="ExternalInput")
    sw1 = dt("sw1", [2, DM, 1408], F32, kind="ExternalInput")
    sb1r = dt("sb1r", [2, 128, NFT], F32, kind="ExternalInput")
    sw2 = dt("sw2", [2, 1408, DM], F32, kind="ExternalInput")
    rw1 = dt("rw1", [16, DM, 1408], F32, kind="ExternalInput")
    rb1r = dt("rb1r", [16, 128, NFT], F32, kind="ExternalInput")
    rw2 = dt("rw2", [16, 1408, DM], F32, kind="ExternalInput")
    b2r = dt("b2r", [16, DM], F32, kind="ExternalInput")
    b2s = dt("b2s", [2, DM], F32, kind="ExternalInput")
    centT = dt("centT", [DM, 16], F32, kind="ExternalInput")
    fbias = dt("fbias", [1, 16], F32, kind="ExternalInput")
    cosT_in = dt("cosT_in", [128, T], F32, kind="ExternalInput")
    sinT_in = dt("sinT_in", [128, T], F32, kind="ExternalInput")
    cmat = dt("cmat", [T, 1024], BF16, kind="ExternalInput")
    masks_in = dt("masks_in", [128, 2048], BF16, kind="ExternalInput")
    ident_in = dt("ident_in", [128, 128], BF16, kind="ExternalInput")
    out_d = dt("out", [1024, DM], F32, kind="ExternalOutput")
    xn_dram = dt("xn_dram", [T, DM], BF16)
    xnm_dram = dt("xnm_dram", [1024, DM], BF16)
    vT_dram = dt("vT_dram", [1024, T], BF16)
    aT_dram = dt("aT_dram", [DM, T], BF16)
    an_dram = dt("an_dram", [1024, DM], BF16)
    g_dram = dt("g_dram", [1024, DM], BF16)

    p = PB()
    es0 = ExitStack()
    sb0 = lambda n, shp, d: es0.enter_context(nc.sbuf_tensor(n, shp, d))
    with es0:
        psA = es0.enter_context(nc.psum_tensor("psA", [128, 1024], F32))
        psB = es0.enter_context(nc.psum_tensor("psB", [128, 1024], F32))
        ps1 = es0.enter_context(nc.psum_tensor("ps1", [128, 512], F32))
        ps2 = es0.enter_context(nc.psum_tensor("ps2", [128, 512], F32))

        masks4 = sb0("masks4", [128, 2048], BF16)
        ident = sb0("ident", [128, 128], BF16)
        ones_bf = sb0("ones_bf", [128, 1], BF16)
        onesrow_f = sb0("onesrow_f", [1, 128], F32)
        onesrow_bf = sb0("onesrow_bf", [1, 128], BF16)
        rmsw_sc = sb0("rmsw_sc", [128, 1], F32)
        lamneg = sb0("lamneg", [1, 1], F32)
        lamv = sb0("lamv", [1, 256], F32)
        lamt = sb0("lamt", [1, 64], F32)
        lam2 = sb0("lam2", [1, 2], F32)
        stat_sum = sb0("stat_sum", [128, 1], F32)
        stat_sq = sb0("stat_sq", [128, 1], F32)
        stat_mean = sb0("stat_mean", [128, 1], F32)
        stat_var = sb0("stat_var", [128, 1], F32)
        stat_rstd = sb0("stat_rstd", [128, 1], F32)
        stat_b2 = sb0("stat_b2", [128, 1], F32)
        b1_all = sb0("b1_all", [128, 18 * NFT], F32)
        b2aug = sb0("b2aug", [32, DM], BF16)
        gt_aug = sb0("gt_aug", [32, 1024], BF16)
        gt_stage = sb0("gt_stage", [1, 1024], BF16)
        biasf_b = sb0("biasf_b", [1, 16], F32)
        rstd16 = sb0("rstd16", [128, 16], F32)
        ms16 = sb0("ms16", [128, 16], F32)

        p.dma("sync", "sq", masks4[:], masks_in[:, :])
        p.dma("sync", "sq", ident[:], ident_in[:, :])
        p.dma("sync", "sq", rmsw_sc[:], rmsw[:, :])
        p.dma("sync", "sq", lamv[:], lam_in[:, :])
        p.dma("sync", "sq", biasf_b[:], fbias[:, :])
        p.dma("sync", "sq", b1_all[:, 16 * NFT:17 * NFT], sb1r[0])
        p.dma("sync", "sq", b1_all[:, 17 * NFT:18 * NFT], sb1r[1])
        for e in range(16):
            p.dma("sync", "sq", b1_all[:, e * NFT:(e + 1) * NFT], rb1r[e])
        sq_consts = p.K["sq"]
        p.dma("gpsimd", "gq", b2aug[0:16, :], b2r[:, :])
        p.dma("gpsimd", "gq", b2aug[16:18, :], b2s[:, :])
        gq_b2 = p.K["gq"]

        p.wait("vector", "sq", sq_consts)
        p.add("vector", lambda s: s.memset(psA[:], 0.0))
        ve_pmem = p.add("vector", lambda s: s.memset(psB[:], 0.0), inc="ve")
        p.wait("tensor", "ve", ve_pmem)
        p.add("vector", lambda s: s.memset(ones_bf[:], 1.0))
        p.add("vector", lambda s: s.memset(onesrow_f[:], 1.0))
        p.add("vector", lambda s: s.memset(onesrow_bf[:], 1.0))
        p.add("vector", lambda s: s.memset(gt_aug[:], 1.0))
        p.add("vector", lambda s: s.tensor_single_scalar(
            rmsw_sc[:], rmsw_sc[:], float(1.0 - L_INIT), ALU.mult))
        p.vsem(lambda s: s.tensor_tensor(
            lamt[:], lamv[0:1, 0:64], lamv[0:1, 64:128], ALU.mult))
        p.vsem(lambda s: s.reduce_sum(lam2[0:1, 0:1], lamt[:], axis=AX.X))
        p.vsem(lambda s: s.tensor_tensor(
            lamt[:], lamv[0:1, 128:192], lamv[0:1, 192:256], ALU.mult))
        ve_l = p.add("vector", lambda s: s.reduce_sum(
            lam2[0:1, 1:2], lamt[:], axis=AX.X), inc="ve")
        p.wait("scalar", "ve", ve_l)
        ac_l = p.add("scalar", lambda s: s.activation(lam2[:], lam2[:], AF.Exp),
                     inc="ac")
        p.wait("vector", "ac", ac_l)
        p.vsem(lambda s: s.tensor_tensor(
            lamneg[:], lam2[0:1, 1:2], lam2[0:1, 0:1], ALU.subtract))
        ve_consts = p.add("vector", lambda s: s.tensor_single_scalar(
            lamneg[:], lamneg[:], float(-L_INIT), ALU.add), inc="ve")

        def emit_ln_tile(xin_tile, out_tile, scr, ve_prev_stats):
            p.wait("scalar", "ve", ve_prev_stats)
            p.add("scalar", lambda s: s.activation(
                scr[:, 0:DM], xin_tile, AF.Identity, accum_out=stat_sum[:]))
            ac_st = p.add("scalar", lambda s: s.activation(
                scr[:, 0:DM], xin_tile, AF.Square, accum_out=stat_sq[:]), inc="ac")
            p.wait("vector", "ac", ac_st)
            p.add("vector", lambda s: s.tensor_single_scalar(
                stat_mean[:], stat_sum[:], 1.0 / DM, ALU.mult))
            p.vsem(lambda s: s.tensor_single_scalar(
                stat_var[:], stat_sq[:], 1.0 / DM, ALU.mult))
            ve_v = p.add("vector", lambda s: s.scalar_tensor_tensor(
                stat_var[:], stat_mean[:], stat_mean[:], stat_var[:],
                ALU.mult, ALU.subtract), inc="ve")
            p.wait("scalar", "ve", ve_v)
            p.asem(lambda s: s.activation(
                stat_rstd[:], stat_var[:], AF.Ln, bias=np.float32(EPS), scale=-1.0))
            ac_r = p.add("scalar", lambda s: s.activation(
                stat_rstd[:], stat_rstd[:], AF.Exp, scale=-0.5), inc="ac")
            p.wait("vector", "ac", ac_r)
            ve_b = p.add("vector", lambda s: s.scalar_tensor_tensor(
                stat_b2[:], stat_mean[:], -1.0, stat_rstd[:], ALU.mult, ALU.mult),
                inc="ve")
            p.wait("scalar", "ve", ve_b)
            ac_x = p.add("scalar", lambda s: s.activation(
                out_tile, xin_tile, AF.Identity, bias=stat_b2[:], scale=stat_rstd[:]),
                inc="ac")
            return ac_x, ve_b

        with ExitStack() as es1:
            scr1 = es1.enter_context(nc.sbuf_tensor("scr1", [128, DM], F32))
            xt = [es1.enter_context(nc.sbuf_tensor(f"xt{i}", [128, DM], F32))
                  for i in range(2)]
            xnb = [es1.enter_context(nc.sbuf_tensor(f"xnb{i}", [128, DM], BF16))
                   for i in range(2)]
            slot_ac = [0, 0]
            slot_sq = [0, 0]
            ve_stats = ve_consts
            for i in range(24):
                sl = i % 2
                if i < 16:
                    src = x_full[128 * i:128 * (i + 1), :]
                    dst = xn_dram[128 * i:128 * (i + 1), :]
                else:
                    src = x_mine[128 * (i - 16):128 * (i - 15), :]
                    dst = xnm_dram[128 * (i - 16):128 * (i - 15), :]
                p.wait("sync", "ac", slot_ac[sl])
                ld = p.dma("sync", "sq", xt[sl][:], src)
                p.wait("scalar", "sq", ld)
                p.wait("scalar", "sq", slot_sq[sl])
                ac_x, ve_stats = emit_ln_tile(xt[sl][:], xnb[sl][:], scr1, ve_stats)
                slot_ac[sl] = ac_x
                p.wait("sync", "ac", ac_x)
                slot_sq[sl] = p.dma("sync", "sq", dst, xnb[sl][:])
            sq_ln = p.K["sq"]

        with ExitStack() as esA:
            nsb = lambda n, shp, d: esA.enter_context(nc.sbuf_tensor(n, shp, d))
            psd = esA.enter_context(nc.psum_tensor("psd", [64, 512], F32))
            psbc = esA.enter_context(nc.psum_tensor("psbc", [128, 512], F32))
            cosT = nsb("cosT", [128, T], F32)
            sinT = nsb("sinT", [128, T], F32)
            xnT = nsb("xnT", [128, 16 * T], BF16)
            v_nat = nsb("v_nat", [128, 16 * 1024], BF16)
            wqkb = [nsb(f"wqkb{i}", [128, 4096], BF16) for i in range(2)]
            wvh = [nsb(f"wvh{i}", [128, 2048], BF16) for i in range(2)]
            vTb = [nsb(f"vTb{i}", [128, T], BF16) for i in range(2)]
            st_f = nsb("st_f", [128, 1024], F32)
            st_sw = nsb("st_sw", [128, 1024], F32)
            qb = [nsb(f"qb{i}", [128, T], BF16) for i in range(2)]
            kb = [nsb(f"kb{i}", [128, T], BF16) for i in range(2)]
            expb = [nsb(f"expb{i}", [128, 1024], BF16) for i in range(2)]
            o1f = nsb("o1f", [128, 512], F32)
            o2f = nsb("o2f", [128, 512], F32)
            t1f = nsb("t1f", [128, 512], F32)
            r1 = nsb("r1", [1, 512], F32)
            r2 = nsb("r2", [1, 512], F32)
            aTc = [nsb(f"aTc{i}", [128, 512], BF16) for i in range(2)]

            p.wait("sync", "sq", sq_ln)
            p.dma("sync", "sq", cosT[:], cosT_in[:, :])
            p.dma("sync", "sq", sinT[:], sinT_in[:, :])
            for k in range(16):
                p.dmat("sync", "sq", xnT[:, k * T:(k + 1) * T],
                       xn_dram[:, 128 * k:128 * (k + 1)])
            sq_xnT = p.K["sq"]

            psAB_ac = [0, 0]
            spair_ac = [0, 0]
            exp_pe = [0, 0]
            qk_pe = [0, 0]
            w_pe = [0, 0]
            wvh_pe = [0, 0]
            vTb_sq = [0, 0]
            aTc_sq = [0, 0]
            o_ac = 0
            d_ve = 0
            bc_ve = 0
            st_ve = 0
            st_ac = [0, 0]
            pe_group = 0
            sq_vnat = 0

            for grp in range(2):
                p.wait("gpsimd", "sq", sq_ln)
                for hh in range(8):
                    h = 8 * grp + hh
                    vsl = hh % 2
                    p.wait("gpsimd", "pe", wvh_pe[vsl])
                    wv3 = wv.ap()[:, 128 * h:128 * (h + 1)].rearrange(
                        "(k p) j -> p k j", p=128)
                    wvd = wvh[vsl][:].rearrange("p (k j) -> p k j", k=16, j=128)
                    p.dma("gpsimd", "gq", wvd[:, 0:8, :], wv3[:, 0:8, :])
                    gq_wv = p.dma("gpsimd", "gq", wvd[:, 8:16, :], wv3[:, 8:16, :])
                    p.wait("tensor", "gq", gq_wv)
                    p.wait("tensor", "sq", sq_xnT)
                    for ch in range(4):
                        slot = ch % 2
                        pdst = (psA, psB)[slot]
                        p.wait("tensor", "ac", psAB_ac[slot])
                        for k in range(16):
                            p.add("tensor", lambda s, pdst=pdst, k=k, ch=ch, vsl=vsl:
                                  s.matmul(pdst[:, 0:512],
                                           wvh[vsl][:, 128 * k:128 * (k + 1)],
                                           xnT[:, k * T + 512 * ch:k * T + 512 * (ch + 1)],
                                           start=(k == 0), stop=(k == 15)),
                                  inc=("pe" if k == 15 else None))
                        pe_v = p.K["pe"]
                        if ch == 3:
                            wvh_pe[vsl] = pe_v
                        p.wait("scalar", "pe", pe_v)
                        p.wait("scalar", "sq", vTb_sq[vsl] if ch == 0 else 0)
                        ac_v = p.add("scalar", lambda s, pdst=pdst, ch=ch, vsl=vsl:
                                     s.activation(vTb[vsl][:, 512 * ch:512 * (ch + 1)],
                                                  pdst[:, 0:512], AF.Identity),
                                     inc="ac")
                        psAB_ac[slot] = ac_v
                    ac_vh = p.K["ac"]
                    p.wait("sync", "ac", ac_vh)
                    p.wait("sync", "sq", sq_vnat)
                    vTb_sq[vsl] = p.dma("sync", "sq",
                                        vT_dram[128 * hh:128 * (hh + 1), :], vTb[vsl][:])
                sq_vT = p.K["sq"]
                p.wait("sync", "sq", sq_vT)
                p.wait("sync", "pe", pe_group)
                for i in range(16):
                    p.dmat("sync", "sq", v_nat[:, i * 1024:(i + 1) * 1024],
                           vT_dram[:, 128 * i:128 * (i + 1)])
                sq_vnat = p.K["sq"]

                for hh in range(8):
                    h = 8 * grp + hh
                    sl = h % 2
                    p.wait("gpsimd", "pe", w_pe[sl])
                    wq3 = wq.ap()[:, 128 * h:128 * (h + 1)].rearrange(
                        "(k p) j -> p k j", p=128)
                    wk3 = wk.ap()[:, 128 * h:128 * (h + 1)].rearrange(
                        "(k p) j -> p k j", p=128)
                    wdst = wqkb[sl][:].rearrange("p (k j) -> p k j", k=32, j=128)
                    p.dma("gpsimd", "gq", wdst[:, 0:8, :], wq3[:, 0:8, :])
                    p.dma("gpsimd", "gq", wdst[:, 8:16, :], wq3[:, 8:16, :])
                    p.dma("gpsimd", "gq", wdst[:, 16:24, :], wk3[:, 0:8, :])
                    p.dma("gpsimd", "gq", wdst[:, 24:32, :], wk3[:, 8:16, :])
                    gq_w = p.K["gq"]

                    for qk in range(2):
                        wof = 2048 * qk
                        dst_b = (qb, kb)[qk][sl]
                        p.wait("tensor", "gq", gq_w)
                        for half in range(2):
                            p.wait("tensor", "ve", st_ve)
                            for ch2 in range(2):
                                ch = 2 * half + ch2
                                slot = ch % 2
                                pdst = (psA, psB)[slot]
                                p.wait("tensor", "ac", psAB_ac[slot])
                                for k in range(16):
                                    p.add("tensor",
                                          lambda s, pdst=pdst, k=k, ch=ch, sl=sl, wof=wof:
                                          s.matmul(pdst[:, 0:512],
                                                   wqkb[sl][:, wof + 128 * k:wof + 128 * (k + 1)],
                                                   xnT[:, k * T + 512 * ch:k * T + 512 * (ch + 1)],
                                                   start=(k == 0), stop=(k == 15)),
                                          inc=("pe" if k == 15 else None))
                                pe_p = p.K["pe"]
                                p.wait("scalar", "pe", pe_p)
                                p.wait("scalar", "ve", st_ve)
                                ac_p = p.add("scalar",
                                             lambda s, pdst=pdst, ch2=ch2:
                                             s.activation(st_f[:, 512 * ch2:512 * (ch2 + 1)],
                                                          pdst[:, 0:512], AF.Identity),
                                             inc="ac")
                                psAB_ac[slot] = ac_p
                            ac_half = p.K["ac"]
                            p.wait("gpsimd", "ac", ac_half)
                            p.dma("gpsimd", "gq", st_sw[0:32, :], st_f[32:64, :])
                            p.dma("gpsimd", "gq", st_sw[32:64, :], st_f[0:32, :])
                            p.dma("gpsimd", "gq", st_sw[64:96, :], st_f[96:128, :])
                            gq_sw = p.dma("gpsimd", "gq", st_sw[96:128, :],
                                          st_f[64:96, :])
                            p.wait("vector", "gq", gq_sw)
                            p.wait("vector", "ve", ve_consts)
                            p.wait("vector", "pe", qk_pe[sl])
                            hof = 1024 * half
                            p.add("vector", lambda s, hof=hof: s.tensor_tensor(
                                st_f[:], st_f[:], cosT[:, hof:hof + 1024], ALU.mult))
                            p.add("vector", lambda s, hof=hof: s.tensor_tensor(
                                st_sw[:], st_sw[:], sinT[:, hof:hof + 1024], ALU.mult))
                            st_ve = p.add("vector",
                                          lambda s, dst_b=dst_b, hof=hof:
                                          s.tensor_tensor(dst_b[:, hof:hof + 1024],
                                                          st_f[:], st_sw[:], ALU.add),
                                          inc="ve")
                    ve_rot = st_ve

                    p.wait("tensor", "ve", ve_rot)
                    p.wait("tensor", "sq", sq_vnat)
                    for tqc in range(4):
                        tq0 = 512 * tqc
                        njs = 4 * tqc + 4
                        for j in reversed(range(njs)):
                            ssl = j % 2
                            spair = (psA, psB)[ssl]
                            p.wait("tensor", "ac", spair_ac[ssl])
                            p.add("tensor",
                                  lambda s, spair=spair, j=j, sl=sl, tq0=tq0:
                                  s.matmul(spair[:, 0:512],
                                           kb[sl][0:64, 128 * j:128 * (j + 1)],
                                           qb[sl][0:64, tq0:tq0 + 512],
                                           start=True, stop=True))
                            pe_s = p.add("tensor",
                                         lambda s, spair=spair, j=j, sl=sl, tq0=tq0:
                                         s.matmul(spair[:, 512:1024],
                                                  kb[sl][64:128, 128 * j:128 * (j + 1)],
                                                  qb[sl][64:128, tq0:tq0 + 512],
                                                  start=True, stop=True), inc="pe")
                            esl = j % 2
                            p.wait("scalar", "pe", pe_s)
                            p.wait("scalar", "pe", exp_pe[esl])
                            ac_e = p.add("scalar", lambda s, spair=spair, esl=esl:
                                         s.activation(expb[esl][:], spair[:], AF.Exp,
                                                      scale=0.125), inc="ac")
                            spair_ac[ssl] = ac_e
                            dep = ("ac", ac_e)
                            if 128 * j >= tq0:
                                oi = j - 4 * tqc
                                p.wait("vector", "ac", ac_e)
                                p.add("vector", lambda s, esl=esl, oi=oi: s.tensor_tensor(
                                    expb[esl][:, 0:512],
                                    expb[esl][:, 0:512],
                                    masks4[:, 512 * oi:512 * (oi + 1)], ALU.mult))
                                ve_m = p.add("vector",
                                             lambda s, esl=esl, oi=oi: s.tensor_tensor(
                                                 expb[esl][:, 512:1024],
                                                 expb[esl][:, 512:1024],
                                                 masks4[:, 512 * oi:512 * (oi + 1)],
                                                 ALU.mult), inc="ve")
                                dep = ("ve", ve_m)
                            p.wait("tensor", dep[0], dep[1])
                            if j == njs - 1:
                                p.wait("tensor", "ac", o_ac)
                                p.wait("tensor", "ve", d_ve)
                            fst = (j == njs - 1)
                            lst = (j == 0)
                            p.add("tensor", lambda s, j=j, hh=hh, esl=esl, fst=fst, lst=lst:
                                  s.matmul(ps1[:, :],
                                           v_nat[:, j * 1024 + 128 * hh:j * 1024 + 128 * (hh + 1)],
                                           expb[esl][:, 0:512],
                                           start=fst, stop=lst))
                            p.add("tensor", lambda s, j=j, hh=hh, esl=esl, fst=fst, lst=lst:
                                  s.matmul(ps2[:, :],
                                           v_nat[:, j * 1024 + 128 * hh:j * 1024 + 128 * (hh + 1)],
                                           expb[esl][:, 512:1024],
                                           start=fst, stop=lst))
                            p.add("tensor", lambda s, esl=esl, fst=fst, lst=lst:
                                  s.matmul(psd[0:1, :], ones_bf[:, 0:1],
                                           expb[esl][:, 0:512],
                                           start=fst, stop=lst,
                                           tile_position=(0, 0), skip_group_check=True))
                            pe_av = p.add("tensor", lambda s, esl=esl, fst=fst, lst=lst:
                                          s.matmul(psd[32:33, :], ones_bf[:, 0:1],
                                                   expb[esl][:, 512:1024],
                                                   start=fst, stop=lst,
                                                   tile_position=(0, 32),
                                                   skip_group_check=True), inc="pe")
                            exp_pe[esl] = pe_av
                        pe_tqc = p.K["pe"]

                        p.wait("vector", "pe", pe_tqc)
                        p.add("vector", lambda s: s.reciprocal(r1[:], psd[0:1, :]))
                        p.vsem(lambda s: s.reciprocal(r2[:], psd[32:33, :]))
                        d_ve = p.add("vector", lambda s: s.tensor_single_scalar(
                            r2[:], r2[:], lamneg[0:1, 0:1], ALU.mult), inc="ve")
                        p.wait("scalar", "pe", pe_tqc)
                        p.add("scalar", lambda s: s.activation(
                            o1f[:], ps1[:, :], AF.Identity))
                        o_ac = p.add("scalar", lambda s: s.activation(
                            o2f[:], ps2[:, :], AF.Identity), inc="ac")
                        p.wait("tensor", "ve", d_ve)
                        p.wait("tensor", "ve", bc_ve)
                        pe_b1 = p.add("tensor", lambda s: s.matmul(
                            psbc[:, :], onesrow_f[0:1, :], r1[:, :],
                            start=True, stop=True), inc="pe")
                        p.wait("vector", "ac", o_ac)
                        p.wait("vector", "pe", pe_b1)
                        ve_t1 = p.add("vector", lambda s: s.tensor_tensor(
                            t1f[:], o1f[:], psbc[:, :], ALU.mult), inc="ve")
                        p.wait("tensor", "ve", ve_t1)
                        pe_b2 = p.add("tensor", lambda s: s.matmul(
                            psbc[:, :], onesrow_f[0:1, :], r2[:, :],
                            start=True, stop=True), inc="pe")
                        asl = tqc % 2
                        p.wait("vector", "pe", pe_b2)
                        p.wait("vector", "sq", aTc_sq[asl])
                        p.vsem(lambda s: s.tensor_tensor(
                            o2f[:], o2f[:], psbc[:, :], ALU.mult))
                        bc_ve = p.add("vector", lambda s, asl=asl: s.tensor_tensor(
                            aTc[asl][:], t1f[:], o2f[:], ALU.add), inc="ve")
                        p.wait("sync", "ve", bc_ve)
                        aTc_sq[asl] = p.dma(
                            "sync", "sq",
                            aT_dram[128 * h:128 * (h + 1), tq0:tq0 + 512], aTc[asl][:])
                    qk_pe[sl] = p.K["pe"]
                    w_pe[sl] = p.K["pe"]
                pe_group = p.K["pe"]
            sq_aT = p.K["sq"]
            attn_done = [("pe", p.K["pe"]), ("ve", p.K["ve"]),
                         ("ac", p.K["ac"]), ("sq", p.K["sq"]), ("gq", p.K["gq"])]

        with ExitStack() as esFo:
            psbc2 = esFo.enter_context(nc.psum_tensor("psbc2", [128, 512], F32))
            pst_g = esFo.enter_context(nc.psum_tensor("pst_g", [16, 128], BF16))
            esY = ExitStack()
            y_acc = esY.enter_context(nc.sbuf_tensor("y_acc", [128, 8 * DM], F32))

            with ExitStack() as esF:
                nsb = lambda n, shp, d: esF.enter_context(nc.sbuf_tensor(n, shp, d))
                xnT_m = nsb("xnT_m", [128, 16 * 1024], BF16)
                hT = nsb("hT", [128, NFT * 1024], BF16)
                w1c = [nsb(f"w1c{i}", [128, 2048], BF16) for i in range(2)]
                w2b = nsb("w2b", [128, NFT * 2048], BF16)
                centb = nsb("centb", [128, 16 * 16], BF16)
                sc_nat = nsb("sc_nat", [128, 16], F32)
                sc_cur = nsb("sc_cur", [128, 16], F32)
                sc_m = nsb("sc_m", [128, 1], F32)
                sc_ge = nsb("sc_ge", [128, 16], F32)
                gb_sb = nsb("gb_sb", [128, 1024], BF16)
                gnat = nsb("gnat", [128, 16], BF16)

                p.barrier_into(("sync", "gpsimd"), attn_done)
                for k in range(16):
                    p.dmat("sync", "sq", xnT_m[:, k * 1024:(k + 1) * 1024],
                           xnm_dram[:, 128 * k:128 * (k + 1)])
                sq_xnm = p.K["sq"]
                cent3 = centT.ap().rearrange("(k p) j -> p k j", p=128)
                p.dma("gpsimd", "gq",
                      centb[:].rearrange("p (k j) -> p k j", k=16, j=16), cent3)
                gq_cent = p.K["gq"]

                p.barrier_into(("tensor",), attn_done)
                p.wait("tensor", "gq", gq_cent)
                p.wait("tensor", "sq", sq_xnm)
                pe_fb = p.add("tensor", lambda s: s.matmul(
                    psbc2[:, 0:16], onesrow_f[0:1, :], biasf_b[0:1, :],
                    start=True, stop=True), inc="pe")
                gt_ac = 0
                gn_ve = 0
                for tt in range(8):
                    for k in range(16):
                        p.add("tensor", lambda s, k=k, tt=tt: s.matmul(
                            ps1[:, 0:16],
                            xnT_m[:, k * 1024 + 128 * tt:k * 1024 + 128 * (tt + 1)],
                            centb[:, 16 * k:16 * (k + 1)],
                            start=(k == 0), stop=(k == 15)),
                            inc=("pe" if k == 15 else None))
                    pe_sc = p.K["pe"]
                    p.wait("scalar", "pe", pe_sc)
                    p.wait("scalar", "ve", gn_ve)
                    ac_sg = p.add("scalar", lambda s: s.activation(
                        sc_nat[:], ps1[:, 0:16], AF.Sigmoid), inc="ac")
                    p.wait("vector", "ac", ac_sg)
                    p.wait("vector", "pe", pe_fb)
                    p.vsem(lambda s: s.tensor_tensor(
                        sc_nat[:], sc_nat[:], psbc2[:, 0:16], ALU.add))
                    p.vsem(lambda s: s.tensor_copy(sc_cur[:], sc_nat[:]))
                    for _ in range(3):
                        p.vsem(lambda s: s.reduce_max(
                            sc_m[:], sc_cur[:], axis=AX.X))
                        p.vsem(lambda s: s.tensor_scalar(
                            sc_ge[:], sc_cur[:], sc_m[:], None, ALU.is_ge))
                        p.vsem(lambda s: s.scalar_tensor_tensor(
                            sc_cur[:], sc_ge[:], -1e30, sc_cur[:], ALU.mult, ALU.add))
                    p.vsem(lambda s: s.reduce_max(
                        sc_m[:], sc_cur[:], axis=AX.X))
                    p.vsem(lambda s: s.tensor_scalar(
                        sc_ge[:], sc_nat[:], sc_m[:], None, ALU.is_ge))
                    p.vsem(lambda s: s.tensor_tensor(
                        sc_ge[:], sc_ge[:], sc_nat[:], ALU.mult))
                    p.vsem(lambda s: s.reduce_sum(
                        sc_m[:], sc_ge[:], axis=AX.X))
                    p.vsem(lambda s: s.tensor_single_scalar(
                        sc_m[:], sc_m[:], 1e-8, ALU.max))
                    p.vsem(lambda s: s.reciprocal(sc_m[:], sc_m[:]))
                    gn_ve = p.add("vector", lambda s: s.tensor_scalar(
                        gnat[:], sc_ge[:], sc_m[:], None, ALU.mult), inc="ve")
                    p.wait("tensor", "ve", gn_ve)
                    p.wait("tensor", "ac", gt_ac)
                    pe_gt = p.add("tensor", lambda s: s.transpose(
                        pst_g[:, :], gnat[:], ident[:]), inc="pe")
                    p.wait("scalar", "pe", pe_gt)
                    p.wait("scalar", "gq", gq_b2)
                    gt_ac = p.add("scalar", lambda s, tt=tt: s.activation(
                        gt_aug[0:16, 128 * tt:128 * (tt + 1)], pst_g[:, :],
                        AF.Identity), inc="ac")

                w1_pe = [0, 0]
                w2_pe = 0
                hT_pe = 0
                gt_pe = 0
                gb_ve = 0
                gb_ac = 0
                ysl_ve = [0, 0]
                order = [16, 17] + list(range(16))
                for ei, e in enumerate(order):
                    shared = e >= 16
                    w1src = (sw1[e - 16] if shared else rw1[e])
                    w2src = (sw2[e - 16] if shared else rw2[e])
                    for ft in range(NFT):
                        slc = ft % 2
                        p.wait("gpsimd", "pe", w1_pe[slc])
                        w13 = w1src[:, 128 * ft:128 * (ft + 1)].rearrange(
                            "(k p) j -> p k j", p=128)
                        w1d = w1c[slc][:].rearrange("p (k j) -> p k j", k=16, j=128)
                        p.dma("gpsimd", "gq", w1d[:, 0:8, :], w13[:, 0:8, :])
                        gq_w1 = p.dma("gpsimd", "gq", w1d[:, 8:16, :], w13[:, 8:16, :])
                        p.wait("tensor", "gq", gq_w1)
                        for tc in range(2):
                            slot = tc
                            pdst = (psA, psB)[slot]
                            p.wait("tensor", "ac", psAB_ac[slot])
                            for k in range(16):
                                p.add("tensor",
                                      lambda s, pdst=pdst, k=k, tc=tc, slc=slc:
                                      s.matmul(pdst[:, 0:512],
                                               w1c[slc][:, 128 * k:128 * (k + 1)],
                                               xnT_m[:, k * 1024 + 512 * tc:k * 1024 + 512 * (tc + 1)],
                                               start=(k == 0), stop=(k == 15)),
                                      inc=("pe" if k == 15 else None))
                            pe_h = p.K["pe"]
                            if tc == 1:
                                w1_pe[slc] = pe_h
                            p.wait("scalar", "pe", pe_h)
                            p.wait("scalar", "ve", gb_ve)
                            ac_h = p.add("scalar",
                                         lambda s, pdst=pdst, ft=ft, tc=tc, e=e:
                                         s.activation(
                                             hT[:, ft * 1024 + 512 * tc:ft * 1024 + 512 * (tc + 1)],
                                             pdst[:, 0:512], AF.Relu,
                                             bias=b1_all[:, e * NFT + ft:e * NFT + ft + 1]),
                                         inc="ac")
                            psAB_ac[slot] = ac_h
                    ac_relu = p.K["ac"]
                    p.wait("gpsimd", "pe", w2_pe)
                    for ft in range(NFT):
                        p.dma("gpsimd", "gq", w2b[:, ft * 2048:(ft + 1) * 2048],
                              w2src[128 * ft:128 * (ft + 1), :])
                    gq_w2 = p.K["gq"]
                    if not shared:
                        p.wait("gpsimd", "ac", gt_ac)
                        p.wait("gpsimd", "pe", gt_pe)
                        gq_gt = p.dma("gpsimd", "gq", gt_stage[:], gt_aug[e:e + 1, :])
                        p.wait("tensor", "gq", gq_gt)
                        p.wait("tensor", "ve", gb_ve)
                        for ch in range(2):
                            p.wait("tensor", "ac", gb_ac)
                            pe_gb = p.add("tensor", lambda s, ch=ch: s.matmul(
                                psbc2[:, :], onesrow_bf[0:1, :],
                                gt_stage[0:1, 512 * ch:512 * (ch + 1)],
                                start=True, stop=True), inc="pe")
                            gt_pe = pe_gb
                            p.wait("scalar", "pe", pe_gb)
                            gb_ac = p.add("scalar", lambda s, ch=ch: s.activation(
                                gb_sb[:, 512 * ch:512 * (ch + 1)], psbc2[:, :],
                                AF.Identity), inc="ac")
                        p.wait("vector", "ac", max(ac_relu, gb_ac))
                        for ft in range(NFT):
                            p.add("vector", lambda s, ft=ft: s.tensor_tensor(
                                hT[:, ft * 1024:(ft + 1) * 1024],
                                hT[:, ft * 1024:(ft + 1) * 1024],
                                gb_sb[:], ALU.mult),
                                inc=("ve" if ft == NFT - 1 else None))
                        gb_ve = p.K["ve"]
                        h_dep = ("ve", gb_ve)
                    else:
                        h_dep = ("ac", ac_relu)
                    p.wait("tensor", h_dep[0], h_dep[1])
                    p.wait("tensor", "gq", gq_w2)
                    for tt in range(8):
                        for dc in range(4):
                            slot = (4 * tt + dc) % 2
                            pdst = (ps1, ps2)[slot]
                            p.wait("tensor", "ve", ysl_ve[slot])
                            if ei == 0 and tt == 0 and dc == 0:
                                p.wait("tensor", "ac", gt_ac)
                            last_is_b2 = (e == 17)
                            for ft in range(NFT):
                                il = (ft == NFT - 1) and not last_is_b2
                                p.add("tensor",
                                      lambda s, pdst=pdst, ft=ft, tt=tt, dc=dc, il=il:
                                      s.matmul(pdst[:, :],
                                               hT[:, ft * 1024 + 128 * tt:ft * 1024 + 128 * (tt + 1)],
                                               w2b[:, ft * 2048 + 512 * dc:ft * 2048 + 512 * (dc + 1)],
                                               start=(ft == 0), stop=il),
                                      inc=("pe" if il else None))
                            if last_is_b2:
                                p.add("tensor", lambda s, pdst=pdst, tt=tt, dc=dc:
                                      s.matmul(pdst[:, :],
                                               gt_aug[0:18, 128 * tt:128 * (tt + 1)],
                                               b2aug[0:18, 512 * dc:512 * (dc + 1)],
                                               start=False, stop=True), inc="pe")
                            pe_y = p.K["pe"]
                            p.wait("vector", "pe", pe_y)
                            col = tt * DM + 512 * dc
                            if ei == 0:
                                ysl_ve[slot] = p.add(
                                    "vector", lambda s, pdst=pdst, col=col:
                                    s.tensor_copy(y_acc[:, col:col + 512], pdst[:, :]),
                                    inc="ve")
                            else:
                                ysl_ve[slot] = p.add(
                                    "vector", lambda s, pdst=pdst, col=col:
                                    s.tensor_tensor(y_acc[:, col:col + 512],
                                                    y_acc[:, col:col + 512],
                                                    pdst[:, :], ALU.add), inc="ve")
                    w2_pe = p.K["pe"]
                    hT_pe = p.K["pe"]
                ve_y = p.K["ve"]
                exp_done = [("pe", p.K["pe"]), ("ve", ve_y), ("ac", p.K["ac"]),
                            ("gq", p.K["gq"])]

            with ExitStack() as es5:
                scr1b = es5.enter_context(nc.sbuf_tensor("scr1b", [128, DM], F32))
                xt5 = [es5.enter_context(nc.sbuf_tensor(f"xt5{i}", [128, DM], F32))
                       for i in range(2)]
                xg = es5.enter_context(nc.sbuf_tensor("xg", [128, DM], F32))
                gsb = [es5.enter_context(nc.sbuf_tensor(f"gsb{i}", [128, DM], BF16))
                       for i in range(2)]
                p.barrier_into(("sync", "scalar", "vector"), exp_done)
                slot_ac5 = [0, 0]
                gsb_sq = [0, 0]
                ve_stats5 = ve_y
                ve_gx = 0
                for tt in range(8):
                    sl5 = tt % 2
                    p.wait("sync", "ac", slot_ac5[sl5])
                    ld = p.dma("sync", "sq", xt5[sl5][:],
                               x_mine[128 * tt:128 * (tt + 1), :])
                    p.wait("scalar", "sq", ld)
                    p.wait("scalar", "ve", ve_gx)
                    ac_x, ve_stats5 = emit_ln_tile(xt5[sl5][:], xg[:], scr1b, ve_stats5)
                    slot_ac5[sl5] = ac_x
                    p.wait("vector", "ac", ac_x)
                    ve_gx = p.add("vector", lambda s, tt=tt: s.tensor_tensor(
                        xg[:], xg[:], y_acc[:, tt * DM:(tt + 1) * DM], ALU.add),
                        inc="ve")
                    p.wait("scalar", "ve", ve_gx)
                    p.wait("scalar", "sq", gsb_sq[sl5])
                    ac_gt5 = p.add("scalar", lambda s, sl5=sl5: s.activation(
                        gsb[sl5][:], xg[:], AF.Silu), inc="ac")
                    p.wait("sync", "ac", ac_gt5)
                    gsb_sq[sl5] = p.dma("sync", "sq",
                                        g_dram[128 * tt:128 * (tt + 1), :], gsb[sl5][:])
            ac_g = p.K["ac"]
            esY.close()
            ffn_done = [("pe", p.K["pe"]), ("ve", p.K["ve"]), ("ac", ac_g),
                        ("gq", p.K["gq"]), ("sq", p.K["sq"])]

            with ExitStack() as esC:
                nsb = lambda n, shp, d: esC.enter_context(nc.sbuf_tensor(n, shp, d))
                a_nat = nsb("a_nat", [128, 16 * DM], BF16)
                cbb = nsb("cbb", [128, 16 * 1024], BF16)
                an_sb = nsb("an_sb", [128, DM], BF16)
                scr6 = nsb("scr6", [128, 128], F32)

                p.barrier_into(("sync",), ffn_done)
                for k in range(16):
                    p.dmat("sync", "sq", a_nat[:, k * DM:(k + 1) * DM],
                           aT_dram[:, 128 * k:128 * (k + 1)])
                for k in range(16):
                    p.dma("sync", "sq", cbb[:, k * 1024:(k + 1) * 1024],
                          cmat[128 * k:128 * (k + 1), :])
                sq_conv = p.K["sq"]

                p.wait("tensor", "sq", sq_conv)
                p.barrier_into(("tensor",), [("ve", ffn_done[1][1]),
                                             ("ac", ac_g)])
                an_sq = 0
                ve_an = 0
                for tt in range(8):
                    p.wait("tensor", "ve", ve_an)
                    for dc in range(4):
                        pdst = (psA, psB)[dc // 2]
                        off = 512 * (dc % 2)
                        for k in range(16):
                            p.add("tensor",
                                  lambda s, pdst=pdst, off=off, k=k, tt=tt, dc=dc:
                                  s.matmul(pdst[:, off:off + 512],
                                           cbb[:, k * 1024 + 128 * tt:k * 1024 + 128 * (tt + 1)],
                                           a_nat[:, k * DM + 512 * dc:k * DM + 512 * (dc + 1)],
                                           start=(k == 0), stop=(k == 15)),
                                  inc=("pe" if k == 15 else None))
                    pe_c = p.K["pe"]
                    p.wait("scalar", "pe", pe_c)
                    p.wait("scalar", "ve", ve_an)
                    for hloc in range(16):
                        pdst = (psA, psB)[hloc // 8]
                        off = 128 * (hloc % 8)
                        bld = (lambda s, pdst=pdst, off=off, hloc=hloc:
                               s.activation(scr6[:], pdst[:, off:off + 128],
                                            AF.Square,
                                            accum_out=ms16[:, hloc:hloc + 1]))
                        if hloc == 15:
                            p.asem(bld)
                        else:
                            p.add("scalar", bld)
                    p.asem(lambda s: s.activation(
                        rstd16[:], ms16[:], AF.Ln, bias=np.float32(EPS), scale=1.0 / 128))
                    ac_rs = p.add("scalar", lambda s: s.activation(
                        rstd16[:], rstd16[:], AF.Exp, scale=-0.5), inc="ac")
                    p.wait("vector", "ac", ac_rs)
                    p.wait("vector", "sq", an_sq)
                    for hloc in range(16):
                        pdst = (psA, psB)[hloc // 8]
                        off = 128 * (hloc % 8)
                        p.add("vector", lambda s, pdst=pdst, off=off, hloc=hloc:
                              s.tensor_scalar(an_sb[:, 128 * hloc:128 * (hloc + 1)],
                                              pdst[:, off:off + 128],
                                              rstd16[:, hloc:hloc + 1], None, ALU.mult),
                              inc=("ve" if hloc == 15 else None))
                    ve_an = p.K["ve"]
                    p.wait("sync", "ve", ve_an)
                    an_sq = p.dma("sync", "sq",
                                  an_dram[128 * tt:128 * (tt + 1), :], an_sb[:])
                conv_done = [("pe", p.K["pe"]), ("ve", p.K["ve"]), ("sq", p.K["sq"])]

            with ExitStack() as esW:
                nsb = lambda n, shp, d: esW.enter_context(nc.sbuf_tensor(n, shp, d))
                woutb = nsb("woutb", [128, 16 * DM], BF16)
                wtmp = [nsb(f"wtmp{i}", [128, DM], F32) for i in range(2)]
                anT = nsb("anT", [128, 16 * 1024], BF16)
                xt7 = [nsb(f"xt7{i}", [128, DM], F32) for i in range(2)]
                outb = [nsb(f"outb{i}", [128, DM], F32) for i in range(2)]
                gld = [nsb(f"gld{i}", [128, DM], BF16) for i in range(2)]
                t7 = nsb("t7", [128, 512], F32)

                p.barrier_into(("sync",), conv_done)
                for k in range(16):
                    p.dmat("sync", "sq", anT[:, k * 1024:(k + 1) * 1024],
                           an_dram[:, 128 * k:128 * (k + 1)])
                sq_anT = p.K["sq"]
                wslot_ve = [0, 0]
                for k in range(16):
                    sl7 = k % 2
                    p.wait("sync", "ve", wslot_ve[sl7])
                    ld = p.dma("sync", "sq", wtmp[sl7][:],
                               wout.ap()[128 * k:128 * (k + 1), :])
                    p.wait("vector", "sq", ld)
                    p.wait("vector", "ve", conv_done[1][1] if k == 0 else 0)
                    wslot_ve[sl7] = p.add("vector",
                                          lambda s, sl7=sl7, k=k: s.tensor_scalar(
                                              woutb[:, k * DM:(k + 1) * DM],
                                              wtmp[sl7][:], rmsw_sc[:], None, ALU.mult),
                                          inc="ve")
                ve_wout = p.K["ve"]

                p.wait("tensor", "sq", sq_anT)
                p.wait("tensor", "ve", ve_wout)
                xslot_ve = [0, 0]
                outb_so = [0, 0]
                psl_ve = [0, 0]
                for tt in range(8):
                    sl7 = tt % 2
                    p.wait("sync", "ve", xslot_ve[sl7])
                    ldx = p.dma("sync", "sq", xt7[sl7][:],
                                x_mine[128 * tt:128 * (tt + 1), :])
                    ldg = p.dma("sync", "sq", gld[sl7][:],
                                g_dram[128 * tt:128 * (tt + 1), :])
                    for dc in range(4):
                        slot = dc % 2
                        pdst = (ps1, ps2)[slot]
                        p.wait("tensor", "ve", psl_ve[slot])
                        for k in range(16):
                            p.add("tensor", lambda s, pdst=pdst, k=k, tt=tt, dc=dc:
                                  s.matmul(pdst[:, :],
                                           anT[:, k * 1024 + 128 * tt:k * 1024 + 128 * (tt + 1)],
                                           woutb[:, k * DM + 512 * dc:k * DM + 512 * (dc + 1)],
                                           start=(k == 0), stop=(k == 15)),
                                  inc=("pe" if k == 15 else None))
                        pe_s7 = p.K["pe"]
                        p.wait("vector", "pe", pe_s7)
                        p.wait("vector", "sq", ldg)
                        p.wait("vector", "so", outb_so[sl7] if dc == 0 else 0)
                        p.vsem(lambda s, pdst=pdst, sl7=sl7, dc=dc:
                               s.tensor_tensor(
                                   t7[:], pdst[:, :],
                                   gld[sl7][:, 512 * dc:512 * (dc + 1)],
                                   ALU.mult))
                        psl_ve[slot] = p.add(
                            "vector", lambda s, sl7=sl7, dc=dc: s.tensor_tensor(
                                outb[sl7][:, 512 * dc:512 * (dc + 1)], t7[:],
                                xt7[sl7][:, 512 * dc:512 * (dc + 1)], ALU.add),
                            inc="ve")
                    xslot_ve[sl7] = max(psl_ve)
                    p.wait("sync", "ve", p.K["ve"])
                    outb_so[sl7] = p.dma("sync", "so",
                                         out_d[128 * tt:128 * (tt + 1), :],
                                         outb[sl7][:])
                p.wait("sync", "so", p.K["so"])

        sems = {}
        with ExitStack() as es_sem:
            for name in ("sq", "gq", "pe", "ac", "ve", "so", "vv", "aa"):
                sems[name] = es_sem.enter_context(nc.semaphore(name))
            with nc.Block() as block:
                p.emit(block, sems)
    return nc


_CACHE = {}


def _prep_inputs(inputs):
    C, CC, SS, perm, masks4, ident = host_constants()
    x = np.ascontiguousarray(np.asarray(inputs["L0_attn_query"], np.float32))
    wq_ = np.ascontiguousarray(np.asarray(inputs["L0_attn_Wq"], np.float32)[:, perm])
    wk_ = np.ascontiguousarray(np.asarray(inputs["L0_attn_Wk"], np.float32)[:, perm])
    sb1 = np.asarray(inputs["L0_ffn_shared_b1"], np.float32)
    rb1 = np.asarray(inputs["L0_ffn_routed_b1"], np.float32)
    shared = dict(
        wq=wq_, wk=wk_,
        wv=np.ascontiguousarray(np.asarray(inputs["L0_attn_Wv"], np.float32)),
        wout=np.ascontiguousarray(np.asarray(inputs["L0_attn_Wout"], np.float32)),
        rmsw=np.asarray(inputs["L0_attn_rms_weight"], np.float32).reshape(128, 1),
        lam_in=np.concatenate(
            [np.asarray(inputs["L0_attn_lambda_q1"], np.float32),
             np.asarray(inputs["L0_attn_lambda_k1"], np.float32),
             np.asarray(inputs["L0_attn_lambda_q2"], np.float32),
             np.asarray(inputs["L0_attn_lambda_k2"], np.float32)]).reshape(1, 256),
        sw1=np.ascontiguousarray(np.asarray(inputs["L0_ffn_shared_w1"], np.float32)),
        sb1r=np.ascontiguousarray(sb1.reshape(2, NFT, 128).transpose(0, 2, 1)),
        sw2=np.ascontiguousarray(np.asarray(inputs["L0_ffn_shared_w2"], np.float32)),
        rw1=np.ascontiguousarray(np.asarray(inputs["L0_ffn_routed_w1"], np.float32)),
        rb1r=np.ascontiguousarray(rb1.reshape(16, NFT, 128).transpose(0, 2, 1)),
        rw2=np.ascontiguousarray(np.asarray(inputs["L0_ffn_routed_w2"], np.float32)),
        b2r=np.ascontiguousarray(np.asarray(inputs["L0_ffn_routed_b2"], np.float32)),
        b2s=np.ascontiguousarray(np.asarray(inputs["L0_ffn_shared_b2"], np.float32)),
        centT=np.ascontiguousarray(
            np.asarray(inputs["L0_ffn_centroids"], np.float32).T),
        fbias=np.asarray(inputs["L0_ffn_bias"], np.float32).reshape(1, 16),
        cosT_in=CC, sinT_in=SS, masks_in=masks4, ident_in=ident,
    )
    in_maps = []
    for c in range(8):
        b, half = c // 2, c % 2
        t0 = 1024 * half
        m = dict(shared)
        m["x_full"] = np.ascontiguousarray(x[b])
        m["x_mine"] = np.ascontiguousarray(x[b, t0:t0 + 1024])
        m["cmat"] = np.ascontiguousarray(C[t0:t0 + 1024, :].T).astype(BF)
        in_maps.append(m)
    return in_maps


def kernel(**inputs):
    if "nc" not in _CACHE:
        _CACHE["nc"] = build_program()
    nc = _CACHE["nc"]
    in_maps = _prep_inputs(inputs)
    res = run_bass_kernel_spmd(nc, in_maps, core_ids=list(range(8)))
    out = np.empty((4, T, DM), np.float32)
    for c in range(8):
        b, half = c // 2, c % 2
        out[b, 1024 * half:1024 * half + 1024] = np.asarray(
            res.results[c]["out"], np.float32)
    return out


# version 14 (final)
# speedup vs baseline: 1.5095x; 1.5095x over previous
import math
import os
from contextlib import ExitStack

ABL = int(os.environ.get("ABL", "0"))

import numpy as np
import ml_dtypes
import concourse.bass as bass
import concourse.mybir as mybir
from concourse.bass_utils import run_bass_kernel_spmd

F32 = mybir.dt.float32
BF16 = mybir.dt.bfloat16
AF = mybir.ActivationFunctionType
ALU = mybir.AluOpType
AX = mybir.AxisListType

T = 2048
DM = 2048
NFT = 11
L_INIT = 0.8 - 0.6 * math.exp(-0.3 * 12)
EPS = 1e-5
BF = ml_dtypes.bfloat16

ENGINES = ("sync", "gpsimd", "tensor", "vector", "scalar")


class PB:

    def __init__(self):
        self.ops = {e: [] for e in ENGINES}
        self.K = {n: 0 for n in ("sq", "gq", "pe", "ac", "ve", "so",
                                 "vv", "aa", "cc")}

    def vsem(self, build):
        val = self.add("vector", build, inc="vv")
        self.wait("vector", "vv", val)
        return val

    def asem(self, build):
        val = self.add("scalar", build, inc="aa")
        self.wait("scalar", "aa", val)
        return val

    def add(self, eng, build, inc=None, amt=1):
        val = None
        if inc is not None:
            self.K[inc] += amt
            val = self.K[inc]

        def closure(sec, sems, build=build, inc=inc, amt=amt):
            ins = build(sec)
            if inc is not None:
                ins.then_inc(sems[inc], amt)
        self.ops[eng].append(closure)
        return val

    def dma(self, eng, sem, out, in_, **kw):
        return self.add(eng, lambda s: s.dma_start(out=out, in_=in_, **kw),
                        inc=sem, amt=16)

    def dmat(self, eng, sem, out, in_):
        return self.add(eng, lambda s: s.dma_start_transpose(out, in_),
                        inc=sem, amt=16)

    def wait(self, eng, sem, val):
        if not val:
            return

        def closure(sec, sems, sem=sem, val=val):
            sec.wait_ge(sems[sem], val)
        self.ops[eng].append(closure)

    def barrier_into(self, engines, conds):
        for e in engines:
            for sem, val in conds:
                self.wait(e, sem, val)

    def emit(self, block, sems):
        for eng in ENGINES:
            ops = self.ops[eng]
            if not ops:
                continue

            def body(section, ops=ops):
                for op in ops:
                    op(section, sems)
            getattr(block, eng)(body)


def host_constants():
    n_freq = T // 2 + 1
    sigma = 0.15
    xs = np.linspace(0.0, 1.0, n_freq)
    filt = np.exp(-0.5 * (xs / sigma) ** 2)
    h = np.fft.irfft(filt, n=T)
    idx = (np.arange(T)[:, None] - np.arange(T)[None, :]) % T
    C = h[idx].astype(np.float32)
    inv_freq = 1.0 / 10000.0 ** (np.arange(0, 64, 2, dtype=np.float64) / 64)
    tt = np.arange(T, dtype=np.float64)
    freqs = np.outer(tt, inv_freq)
    cos, sin = np.cos(freqs), np.sin(freqs)
    pmod = np.arange(128) % 32
    CC = np.ascontiguousarray(cos[:, pmod].T).astype(np.float32)
    sgn = np.where((np.arange(128) % 64) < 32, -1.0, 1.0)
    SS = np.ascontiguousarray((sin[:, pmod].T) * sgn[:, None]).astype(np.float32)
    perm = np.empty(2048, dtype=np.int64)
    for g in range(32):
        b0 = 64 * g
        perm[b0:b0 + 32] = b0 + 2 * np.arange(32)
        perm[b0 + 32:b0 + 64] = b0 + 2 * np.arange(32) + 1
    pp = np.arange(128)[:, None]
    col = np.arange(512)[None, :]
    masks4 = np.concatenate(
        [(col >= (128 * oi + pp)).astype(np.float32) for oi in range(4)],
        axis=1).astype(BF)
    ident = np.eye(128, dtype=np.float32).astype(BF)
    return C, CC, SS, perm, masks4, ident


def build_program():
    nc = bass.Bass("TRN2", target_bir_lowering=False, detect_race_conditions=False)
    dt = nc.dram_tensor
    x_full = dt("x_full", [T, DM], F32, kind="ExternalInput")
    x_mine = dt("x_mine", [1024, DM], F32, kind="ExternalInput")
    wq = dt("wq", [DM, 1024], F32, kind="ExternalInput")
    wk = dt("wk", [DM, 1024], F32, kind="ExternalInput")
    wv = dt("wv", [DM, 1024], F32, kind="ExternalInput")
    wout = dt("wout", [DM, DM], F32, kind="ExternalInput")
    rmsw = dt("rmsw", [128, 1], F32, kind="ExternalInput")
    lam_in = dt("lam_in", [1, 256], F32, kind="ExternalInput")
    sw1 = dt("sw1", [2, DM, 1408], F32, kind="ExternalInput")
    sb1r = dt("sb1r", [2, 128, NFT], F32, kind="ExternalInput")
    sw2 = dt("sw2", [2, 1408, DM], F32, kind="ExternalInput")
    rw1 = dt("rw1", [16, DM, 1408], F32, kind="ExternalInput")
    rb1r = dt("rb1r", [16, 128, NFT], F32, kind="ExternalInput")
    rw2 = dt("rw2", [16, 1408, DM], F32, kind="ExternalInput")
    b2r = dt("b2r", [16, DM], F32, kind="ExternalInput")
    b2s = dt("b2s", [2, DM], F32, kind="ExternalInput")
    centT = dt("centT", [DM, 16], F32, kind="ExternalInput")
    fbias = dt("fbias", [1, 16], F32, kind="ExternalInput")
    cosT_in = dt("cosT_in", [128, T], F32, kind="ExternalInput")
    sinT_in = dt("sinT_in", [128, T], F32, kind="ExternalInput")
    cmat = dt("cmat", [T, 1024], BF16, kind="ExternalInput")
    masks_in = dt("masks_in", [128, 2048], BF16, kind="ExternalInput")
    ident_in = dt("ident_in", [128, 128], BF16, kind="ExternalInput")
    identf_in = dt("identf_in", [128, 128], F32, kind="ExternalInput")
    out_d = dt("out", [1024, DM], F32, kind="ExternalOutput")
    xn_dram = dt("xn_dram", [T, DM], BF16)
    xnm_dram = dt("xnm_dram", [1024, DM], BF16)
    a_dram = dt("a_dram", [T, 1024], BF16)
    a_gath = dt("a_gath", [2 * T, 1024], BF16)
    an_dram = dt("an_dram", [1024, DM], BF16)
    g_dram = dt("g_dram", [1024, DM], BF16)

    p = PB()
    es0 = ExitStack()
    sb0 = lambda n, shp, d: es0.enter_context(nc.sbuf_tensor(n, shp, d))
    with es0:
        psA = es0.enter_context(nc.psum_tensor("psA", [128, 1024], F32))
        psB = es0.enter_context(nc.psum_tensor("psB", [128, 1024], F32))

        masks4 = sb0("masks4", [128, 2048], BF16)
        ident = sb0("ident", [128, 128], BF16)
        identf = sb0("identf", [128, 128], F32)
        ones_bf = sb0("ones_bf", [128, 1], BF16)
        onesrow_f = sb0("onesrow_f", [1, 128], F32)
        onesrow_bf = sb0("onesrow_bf", [1, 128], BF16)
        rmsw_sc = sb0("rmsw_sc", [128, 1], F32)
        lamneg = sb0("lamneg", [1, 1], F32)
        lamv = sb0("lamv", [1, 256], F32)
        lamt = sb0("lamt", [1, 64], F32)
        lam2 = sb0("lam2", [1, 2], F32)
        stat_sum = sb0("stat_sum", [128, 1], F32)
        stat_sq = sb0("stat_sq", [128, 1], F32)
        stat_mean = sb0("stat_mean", [128, 1], F32)
        stat_var = sb0("stat_var", [128, 1], F32)
        stat_rstd = sb0("stat_rstd", [128, 1], F32)
        stat_b2 = sb0("stat_b2", [128, 1], F32)
        b1_all = sb0("b1_all", [128, 18 * NFT], F32)
        b2aug = sb0("b2aug", [32, DM], BF16)
        gt_aug = sb0("gt_aug", [32, 1024], BF16)
        gt_stage = sb0("gt_stage", [1, 1024], BF16)
        biasf_b = sb0("biasf_b", [1, 16], F32)
        rstd16 = sb0("rstd16", [128, 16], F32)
        ms16 = sb0("ms16", [128, 16], F32)

        p.dma("sync", "sq", masks4[:], masks_in[:, :])
        p.dma("sync", "sq", ident[:], ident_in[:, :])
        p.dma("sync", "sq", identf[:], identf_in[:, :])
        p.dma("sync", "sq", rmsw_sc[:], rmsw[:, :])
        p.dma("sync", "sq", lamv[:], lam_in[:, :])
        p.dma("sync", "sq", biasf_b[:], fbias[:, :])
        p.dma("sync", "sq", b1_all[:, 16 * NFT:17 * NFT], sb1r[0])
        p.dma("sync", "sq", b1_all[:, 17 * NFT:18 * NFT], sb1r[1])
        for e in range(16):
            p.dma("sync", "sq", b1_all[:, e * NFT:(e + 1) * NFT], rb1r[e])
        sq_consts = p.K["sq"]
        p.dma("gpsimd", "gq", b2aug[0:16, :], b2r[:, :])
        p.dma("gpsimd", "gq", b2aug[16:18, :], b2s[:, :])
        gq_b2 = p.K["gq"]

        p.wait("vector", "sq", sq_consts)
        p.add("vector", lambda s: s.memset(psA[:], 0.0))
        ve_pmem = p.add("vector", lambda s: s.memset(psB[:], 0.0), inc="ve")
        p.wait("tensor", "ve", ve_pmem)
        p.add("vector", lambda s: s.memset(ones_bf[:], 1.0))
        p.add("vector", lambda s: s.memset(onesrow_f[:], 1.0))
        p.add("vector", lambda s: s.memset(onesrow_bf[:], 1.0))
        p.add("vector", lambda s: s.memset(gt_aug[:], 1.0))
        p.add("vector", lambda s: s.tensor_single_scalar(
            rmsw_sc[:], rmsw_sc[:], float(1.0 - L_INIT), ALU.mult))
        p.vsem(lambda s: s.tensor_tensor(
            lamt[:], lamv[0:1, 0:64], lamv[0:1, 64:128], ALU.mult))
        p.vsem(lambda s: s.reduce_sum(lam2[0:1, 0:1], lamt[:], axis=AX.X))
        p.vsem(lambda s: s.tensor_tensor(
            lamt[:], lamv[0:1, 128:192], lamv[0:1, 192:256], ALU.mult))
        ve_l = p.add("vector", lambda s: s.reduce_sum(
            lam2[0:1, 1:2], lamt[:], axis=AX.X), inc="ve")
        p.wait("scalar", "ve", ve_l)
        ac_l = p.add("scalar", lambda s: s.activation(lam2[:], lam2[:], AF.Exp),
                     inc="ac")
        p.wait("vector", "ac", ac_l)
        p.vsem(lambda s: s.tensor_tensor(
            lamneg[:], lam2[0:1, 1:2], lam2[0:1, 0:1], ALU.subtract))
        ve_consts = p.add("vector", lambda s: s.tensor_single_scalar(
            lamneg[:], lamneg[:], float(-L_INIT), ALU.add), inc="ve")

        def emit_ln_tile(xin_tile, out_tile, scr, ve_prev_stats):
            p.wait("scalar", "ve", ve_prev_stats)
            p.add("scalar", lambda s: s.activation(
                scr[:, 0:DM], xin_tile, AF.Identity, accum_out=stat_sum[:]))
            ac_st = p.add("scalar", lambda s: s.activation(
                scr[:, 0:DM], xin_tile, AF.Square, accum_out=stat_sq[:]), inc="ac")
            p.wait("vector", "ac", ac_st)
            p.add("vector", lambda s: s.tensor_single_scalar(
                stat_mean[:], stat_sum[:], 1.0 / DM, ALU.mult))
            p.vsem(lambda s: s.tensor_single_scalar(
                stat_var[:], stat_sq[:], 1.0 / DM, ALU.mult))
            ve_v = p.add("vector", lambda s: s.scalar_tensor_tensor(
                stat_var[:], stat_mean[:], stat_mean[:], stat_var[:],
                ALU.mult, ALU.subtract), inc="ve")
            p.wait("scalar", "ve", ve_v)
            p.asem(lambda s: s.activation(
                stat_rstd[:], stat_var[:], AF.Ln, bias=np.float32(EPS), scale=-1.0))
            ac_r = p.add("scalar", lambda s: s.activation(
                stat_rstd[:], stat_rstd[:], AF.Exp, scale=-0.5), inc="ac")
            p.wait("vector", "ac", ac_r)
            ve_b = p.add("vector", lambda s: s.scalar_tensor_tensor(
                stat_b2[:], stat_mean[:], -1.0, stat_rstd[:], ALU.mult, ALU.mult),
                inc="ve")
            p.wait("scalar", "ve", ve_b)
            ac_x = p.add("scalar", lambda s: s.activation(
                out_tile, xin_tile, AF.Identity, bias=stat_b2[:], scale=stat_rstd[:]),
                inc="ac")
            return ac_x, ve_b

        with ExitStack() as es1:
            scr1 = es1.enter_context(nc.sbuf_tensor("scr1", [128, DM], F32))
            xt = [es1.enter_context(nc.sbuf_tensor(f"xt{i}", [128, DM], F32))
                  for i in range(2)]
            xnb = [es1.enter_context(nc.sbuf_tensor(f"xnb{i}", [128, DM], BF16))
                   for i in range(2)]
            slot_ac = [0, 0]
            slot_sq = [0, 0]
            ve_stats = ve_consts
            for i in range(24):
                sl = i % 2
                if i < 16:
                    src = x_full[128 * i:128 * (i + 1), :]
                    dst = xn_dram[128 * i:128 * (i + 1), :]
                else:
                    src = x_mine[128 * (i - 16):128 * (i - 15), :]
                    dst = xnm_dram[128 * (i - 16):128 * (i - 15), :]
                p.wait("sync", "ac", slot_ac[sl])
                ld = p.dma("sync", "sq", xt[sl][:], src)
                p.wait("scalar", "sq", ld)
                p.wait("scalar", "sq", slot_sq[sl])
                ac_x, ve_stats = emit_ln_tile(xt[sl][:], xnb[sl][:], scr1, ve_stats)
                slot_ac[sl] = ac_x
                p.wait("sync", "ac", ac_x)
                slot_sq[sl] = p.dma("sync", "sq", dst, xnb[sl][:])
            sq_ln = p.K["sq"]

        with ExitStack() as esA:
            nsb = lambda n, shp, d: esA.enter_context(nc.sbuf_tensor(n, shp, d))
            oT = [esA.enter_context(nc.psum_tensor("oTa", [128, 512], F32)),
                  esA.enter_context(nc.psum_tensor("oTb", [128, 512], F32)),
                  esA.enter_context(nc.psum_tensor("oTc", [128, 512], F32))]
            oslice = [(oT[i // 3], 129 * (i % 3)) for i in range(8)]
            cosT = nsb("cosT", [128, T], F32)
            sinT = nsb("sinT", [128, T], F32)
            xnT = nsb("xnT", [128, 16 * T], BF16)
            v_aug = nsb("v_aug", [128, 8 * 16 * 129], BF16)
            wvh_all = nsb("wvh_all", [128, 16 * 1024], BF16)
            wqkb = [nsb(f"wqkb{i}", [128, 4096], BF16) for i in range(2)]
            qb = [nsb(f"qb{i}", [128, T], BF16) for i in range(2)]
            kb = [nsb(f"kb{i}", [128, T], BF16) for i in range(2)]
            expb = [nsb(f"expb{i}", [128, 1024], BF16) for i in range(2)]
            stc = [nsb(f"stc{i}", [128, 512], BF16) for i in range(2)]
            stw = [nsb(f"stw{i}", [128, 512], BF16) for i in range(2)]
            tm1 = [nsb(f"tm1{i}", [128, 512], BF16) for i in range(2)]
            aSt = [nsb(f"aSt{i}", [128, 512], BF16) for i in range(2)]
            r_sb = nsb("r_sb", [128, 8], F32)
            tmp_sb = nsb("tmp_sb", [128, 512], F32)
            lamneg_bc = nsb("lamneg_bc", [128, 1], F32)

            p.wait("tensor", "ve", ve_consts)
            pe_lbc = p.add("tensor", lambda s: s.matmul(
                psA[:, 0:1], onesrow_f[0:1, :], lamneg[0:1, 0:1],
                start=True, stop=True), inc="pe")
            p.wait("scalar", "pe", pe_lbc)
            ac_lbc = p.add("scalar", lambda s: s.activation(
                lamneg_bc[:], psA[:, 0:1], AF.Identity), inc="ac")

            ve_vm = p.add("vector", lambda s: s.memset(v_aug[:], 1.0), inc="ve")

            p.wait("sync", "sq", sq_ln)
            p.dma("sync", "sq", cosT[:], cosT_in[:, :])
            p.dma("sync", "sq", sinT[:], sinT_in[:, :])
            for k in range(16):
                p.dmat("sync", "sq", xnT[:, k * T:(k + 1) * T],
                       xn_dram[:, 128 * k:128 * (k + 1)])
            sq_xnT = p.K["sq"]

            free_ac = [ac_lbc, 0]
            free_ve = [0, 0]
            exp_pe = [0, 0]
            w_pe = [0, 0]
            wv_pe = 0
            ep_ve = 0
            aSt_sq = [0, 0]
            stc_gq = [0, 0]
            rot_done = [0, 0]
            pe_group = 0

            for grp in range(1):
                p.wait("gpsimd", "pe", wv_pe)
                wv3 = wv.ap()[:, 0:1024].rearrange(
                    "(k p) j -> p k j", p=128)
                wvd = wvh_all[:].rearrange("p (k j) -> p k j", k=16, j=1024)
                p.dma("gpsimd", "gq", wvd[:, 0:8, :], wv3[:, 0:8, :])
                gq_wv = p.dma("gpsimd", "gq", wvd[:, 8:16, :], wv3[:, 8:16, :])

                p.wait("tensor", "gq", gq_wv)
                p.wait("tensor", "sq", sq_xnT)
                for tch in range(16):
                    slot = tch % 2
                    pdst = (psA, psB)[slot]
                    p.wait("tensor", "ac", free_ac[slot])
                    p.wait("tensor", "ve", free_ve[slot])
                    for dh in range(2):
                        for k in range(16):
                            p.add("tensor",
                                  lambda s, pdst=pdst, k=k, tch=tch, dh=dh:
                                  s.matmul(pdst[:, 512 * dh:512 * (dh + 1)],
                                           xnT[:, k * T + 128 * tch:k * T + 128 * (tch + 1)],
                                           wvh_all[:, 1024 * k + 512 * dh:1024 * k + 512 * (dh + 1)],
                                           start=(k == 0), stop=(k == 15)),
                                  inc=("pe" if (dh == 1 and k == 15) else None))
                    pe_v = p.K["pe"]
                    p.wait("scalar", "pe", pe_v)
                    p.wait("scalar", "pe", pe_group)
                    p.wait("scalar", "ve", ve_vm)
                    for hh in range(8):
                        col = (hh * 16 + tch) * 129
                        p.add("scalar", lambda s, pdst=pdst, hh=hh, col=col:
                              s.activation(v_aug[:, col:col + 128],
                                           pdst[:, 128 * hh:128 * (hh + 1)],
                                           AF.Identity),
                              inc=("ac" if hh == 7 else None))
                    free_ac[slot] = p.K["ac"]
                ac_vgrp = p.K["ac"]
                wv_pe = p.K["pe"]

                for hh in range(8):
                    h = hh
                    sl = h % 2
                    p.wait("gpsimd", "pe", w_pe[sl])
                    wq3 = wq.ap()[:, 128 * h:128 * (h + 1)].rearrange(
                        "(k p) j -> p k j", p=128)
                    wk3 = wk.ap()[:, 128 * h:128 * (h + 1)].rearrange(
                        "(k p) j -> p k j", p=128)
                    wdst = wqkb[sl][:].rearrange("p (k j) -> p k j", k=32, j=128)
                    p.dma("gpsimd", "gq", wdst[:, 0:8, :], wq3[:, 0:8, :])
                    p.dma("gpsimd", "gq", wdst[:, 8:16, :], wq3[:, 8:16, :])
                    p.dma("gpsimd", "gq", wdst[:, 16:24, :], wk3[:, 0:8, :])
                    gq_w = p.dma("gpsimd", "gq", wdst[:, 24:32, :], wk3[:, 8:16, :])

                    for qk in range(0 if ABL >= 2 else 2):
                        wof = 2048 * qk
                        dst_b = (qb, kb)[qk][sl]
                        for ch in range(4):
                            ci = 4 * qk + ch
                            slot = ci % 2
                            csl = ci % 2
                            off = 512 * ch
                            pdst = (psA, psB)[slot]
                            p.wait("tensor", "gq", gq_w)
                            p.wait("tensor", "ac", free_ac[slot])
                            p.wait("tensor", "ve", free_ve[slot])
                            for k in range(16):
                                p.add("tensor",
                                      lambda s, pdst=pdst, k=k, off=off, sl=sl, wof=wof:
                                      s.matmul(pdst[:, 0:512],
                                               wqkb[sl][:, wof + 128 * k:wof + 128 * (k + 1)],
                                               xnT[:, k * T + off:k * T + off + 512],
                                               start=(k == 0), stop=(k == 15)),
                                      inc=("pe" if k == 15 else None))
                            pe_q = p.K["pe"]
                            p.wait("vector", "pe", pe_q)
                            p.wait("vector", "gq", stc_gq[csl])
                            p.wait("vector", "ve", rot_done[csl])
                            ve_cp = p.add("vector", lambda s, pdst=pdst, csl=csl:
                                          s.tensor_copy(stc[csl][:],
                                                        pdst[:, 0:512]), inc="ve")
                            free_ve[slot] = p.add(
                                "vector", lambda s, pdst=pdst, csl=csl, off=off:
                                s.tensor_tensor(tm1[csl][:], pdst[:, 0:512],
                                                cosT[:, off:off + 512], ALU.mult),
                                inc="ve")
                            p.wait("gpsimd", "ve", ve_cp)
                            p.dma("gpsimd", "gq", stw[csl][0:32, :], stc[csl][32:64, :])
                            p.dma("gpsimd", "gq", stw[csl][32:64, :], stc[csl][0:32, :])
                            p.dma("gpsimd", "gq", stw[csl][64:96, :], stc[csl][96:128, :])
                            stc_gq[csl] = p.dma("gpsimd", "gq", stw[csl][96:128, :],
                                                stc[csl][64:96, :])
                            p.wait("vector", "gq", stc_gq[csl])
                            p.vsem(lambda s, csl=csl, off=off: s.tensor_tensor(
                                stw[csl][:], stw[csl][:], sinT[:, off:off + 512],
                                ALU.mult))
                            rot_done[csl] = p.add(
                                "vector", lambda s, dst_b=dst_b, csl=csl, off=off:
                                s.tensor_tensor(dst_b[:, off:off + 512],
                                                tm1[csl][:], stw[csl][:], ALU.add),
                                inc="ve")
                    ve_rot = p.K["ve"]

                    p.wait("tensor", "ve", ve_rot)
                    p.wait("tensor", "ac", ac_vgrp)

                    def emit_scores(tqc, j):
                        ssl = j % 2
                        spair = (psA, psB)[ssl]
                        tq0 = 512 * tqc
                        p.wait("tensor", "ac", free_ac[ssl])
                        p.wait("tensor", "ve", free_ve[ssl])
                        p.add("tensor",
                              lambda s, spair=spair, j=j, sl=sl, tq0=tq0:
                              s.matmul(spair[:, 0:512],
                                       kb[sl][0:64, 128 * j:128 * (j + 1)],
                                       qb[sl][0:64, tq0:tq0 + 512],
                                       start=True, stop=True))
                        return p.add("tensor",
                                     lambda s, spair=spair, j=j, sl=sl, tq0=tq0:
                                     s.matmul(spair[:, 512:1024],
                                              kb[sl][64:128, 128 * j:128 * (j + 1)],
                                              qb[sl][64:128, tq0:tq0 + 512],
                                              start=True, stop=True), inc="pe")

                    def emit_exp(tqc, j, pe_s):
                        ssl = j % 2
                        esl = j % 2
                        spair = (psA, psB)[ssl]
                        tq0 = 512 * tqc
                        p.wait("scalar", "pe", pe_s)
                        p.wait("scalar", "pe", exp_pe[esl])
                        ac_e = p.add("scalar", lambda s, spair=spair, esl=esl:
                                     s.activation(expb[esl][:], spair[:], AF.Exp,
                                                  scale=0.125), inc="ac")
                        free_ac[ssl] = ac_e
                        dep = ("ac", ac_e)
                        if 128 * j >= tq0:
                            oi = j - 4 * tqc
                            p.wait("vector", "ac", ac_e)
                            p.add("vector", lambda s, esl=esl, oi=oi: s.tensor_tensor(
                                expb[esl][:, 0:512],
                                expb[esl][:, 0:512],
                                masks4[:, 512 * oi:512 * (oi + 1)], ALU.mult))
                            ve_m = p.add("vector",
                                         lambda s, esl=esl, oi=oi: s.tensor_tensor(
                                             expb[esl][:, 512:1024],
                                             expb[esl][:, 512:1024],
                                             masks4[:, 512 * oi:512 * (oi + 1)],
                                             ALU.mult), inc="ve")
                            dep = ("ve", ve_m)
                        return [dep, dep]

                    def emit_av(tqc, j, deps):
                        esl = j % 2
                        njs = 4 * tqc + 4
                        fst = (j == njs - 1)
                        lst = (j == 0)
                        vcol = (hh * 16 + j) * 129
                        for halfq in range(2):
                            p.wait("tensor", deps[halfq][0], deps[halfq][1])
                            if fst and halfq == 0:
                                p.wait("tensor", "ve", ep_ve)
                            for qc in range(4):
                                i = 4 * halfq + qc
                                ot, ooff = oslice[i]
                                st = fst and (i % 3 == 0)
                                p.add("tensor",
                                      lambda s, ot=ot, ooff=ooff, esl=esl,
                                      halfq=halfq, qc=qc, vcol=vcol,
                                      st=st, lst=lst:
                                      s.matmul(ot[:, ooff:ooff + 129],
                                               expb[esl][:, 512 * halfq + 128 * qc:512 * halfq + 128 * (qc + 1)],
                                               v_aug[:, vcol:vcol + 129],
                                               start=st, stop=lst,
                                               skip_group_check=True),
                                      inc=("pe" if (halfq == 1 and qc == 3)
                                           else None))
                        exp_pe[esl] = p.K["pe"]

                    def emit_epilogue(tqc):
                        nonlocal ep_ve
                        tq0 = 512 * tqc
                        pe_tqc = p.K["pe"]
                        asl = tqc % 2
                        p.wait("vector", "pe", pe_tqc)
                        p.wait("vector", "ac", ac_lbc)
                        p.wait("vector", "sq", aSt_sq[asl])
                        for i in range(8):
                            ot, ooff = oslice[i]
                            bld = (lambda s, ot=ot, ooff=ooff, i=i: s.reciprocal(
                                r_sb[:, i:i + 1], ot[:, ooff + 128:ooff + 129]))
                            if i == 7:
                                p.vsem(bld)
                            else:
                                p.add("vector", bld)
                        p.vsem(lambda s: s.tensor_scalar(
                            r_sb[:, 4:8], r_sb[:, 4:8], lamneg_bc[:], None,
                            ALU.mult))
                        for qc in range(4):
                            i = 4 + qc
                            ot, ooff = oslice[i]
                            bld = (lambda s, ot=ot, ooff=ooff, i=i, qc=qc:
                                   s.tensor_scalar(tmp_sb[:, 128 * qc:128 * (qc + 1)],
                                                   ot[:, ooff:ooff + 128],
                                                   r_sb[:, i:i + 1], None, ALU.mult))
                            if qc == 3:
                                p.vsem(bld)
                            else:
                                p.add("vector", bld)
                        for qc in range(4):
                            ot, ooff = oslice[qc]
                            p.add("vector",
                                  lambda s, ot=ot, ooff=ooff, qc=qc, asl=asl:
                                  s.scalar_tensor_tensor(
                                      aSt[asl][:, 128 * qc:128 * (qc + 1)],
                                      ot[:, ooff:ooff + 128],
                                      r_sb[:, qc:qc + 1],
                                      tmp_sb[:, 128 * qc:128 * (qc + 1)],
                                      ALU.mult, ALU.add),
                                  inc=("ve" if qc == 3 else None))
                        ep_ve = p.K["ve"]
                        p.wait("sync", "ve", ep_ve)
                        for qc in range(4):
                            p.dma("sync", "sq",
                                  a_dram[tq0 + 128 * qc:tq0 + 128 * (qc + 1),
                                         128 * h:128 * (h + 1)],
                                  aSt[asl][:, 128 * qc:128 * (qc + 1)])
                        aSt_sq[asl] = p.K["sq"]

                    seq = ([] if ABL >= 1 else
                           [(tqc, j) for tqc in range(4)
                            for j in reversed(range(4 * tqc + 4))])
                    pe_s_map = {}
                    if seq:
                        pe_s_map[seq[0]] = emit_scores(*seq[0])
                    for idx, (tqc, j) in enumerate(seq):
                        deps = emit_exp(tqc, j, pe_s_map[(tqc, j)])
                        if idx + 1 < len(seq):
                            pe_s_map[seq[idx + 1]] = emit_scores(*seq[idx + 1])
                        emit_av(tqc, j, deps)
                        if j == 0:
                            emit_epilogue(tqc)
                    w_pe[sl] = p.K["pe"]
                pe_group = p.K["pe"]
            attn_done = [("pe", p.K["pe"]), ("ve", p.K["ve"]),
                         ("ac", p.K["ac"]), ("sq", p.K["sq"]), ("gq", p.K["gq"])]

        with ExitStack() as esFo:
            psbc2 = esFo.enter_context(nc.psum_tensor("psbc2", [128, 512], F32))
            pst_g = esFo.enter_context(nc.psum_tensor("pst_g", [16, 128], F32))
            ps1 = esFo.enter_context(nc.psum_tensor("ps1", [128, 512], F32))
            ps2 = esFo.enter_context(nc.psum_tensor("ps2", [128, 512], F32))
            esY = ExitStack()
            y_acc = esY.enter_context(nc.sbuf_tensor("y_acc", [128, 8 * DM], F32))

            with ExitStack() as esF:
                nsb = lambda n, shp, d: esF.enter_context(nc.sbuf_tensor(n, shp, d))
                xnT_m = nsb("xnT_m", [128, 16 * 1024], BF16)
                hT = nsb("hT", [128, NFT * 1024], BF16)
                w1c = [nsb(f"w1c{i}", [128, 2048], BF16) for i in range(2)]
                w2b = nsb("w2b", [128, NFT * 2048], BF16)
                centb = nsb("centb", [128, 16 * 16], BF16)
                sc_nat = nsb("sc_nat", [128, 16], F32)
                sc_cur = nsb("sc_cur", [128, 16], F32)
                sc_m = nsb("sc_m", [128, 1], F32)
                sc_ge = nsb("sc_ge", [128, 16], F32)
                gall = nsb("gall", [128, 8 * 16], F32)

                p.barrier_into(("sync", "gpsimd"), attn_done)
                if ABL >= 3:
                    p.dma("gpsimd", "cc", a_gath[0:T, :], a_dram[:, :])
                    cc_gq = p.dma("gpsimd", "cc", a_gath[T:2 * T, :],
                                  a_dram[:, :])
                else:
                    cc_gq = p.add("gpsimd", lambda s: s.collective_compute(
                        "AllGather", ALU.bypass,
                        replica_groups=[[0, 1], [2, 3], [4, 5], [6, 7]],
                        ins=[a_dram.ap()], outs=[a_gath.ap()]),
                        inc="cc", amt=1)
                    p.barrier_into(("gpsimd", "sync", "tensor", "vector",
                                    "scalar"), [("cc", cc_gq)])
                for k in range(16):
                    p.dmat("sync", "sq", xnT_m[:, k * 1024:(k + 1) * 1024],
                           xnm_dram[:, 128 * k:128 * (k + 1)])
                sq_xnm = p.K["sq"]
                cent3 = centT.ap().rearrange("(k p) j -> p k j", p=128)
                p.dma("gpsimd", "gq",
                      centb[:].rearrange("p (k j) -> p k j", k=16, j=16), cent3)
                gq_cent = p.K["gq"]

                p.barrier_into(("tensor",), attn_done)
                p.wait("tensor", "gq", gq_cent)
                p.wait("tensor", "sq", sq_xnm)
                pe_fb = p.add("tensor", lambda s: s.matmul(
                    psbc2[:, 0:16], onesrow_f[0:1, :], biasf_b[0:1, :],
                    start=True, stop=True), inc="pe")
                gt_ac = 0
                gn_ve = 0
                for tt in range(8):
                    for k in range(16):
                        p.add("tensor", lambda s, k=k, tt=tt: s.matmul(
                            ps1[:, 0:16],
                            xnT_m[:, k * 1024 + 128 * tt:k * 1024 + 128 * (tt + 1)],
                            centb[:, 16 * k:16 * (k + 1)],
                            start=(k == 0), stop=(k == 15)),
                            inc=("pe" if k == 15 else None))
                    pe_sc = p.K["pe"]
                    p.wait("scalar", "pe", pe_sc)
                    p.wait("scalar", "ve", gn_ve)
                    ac_sg = p.add("scalar", lambda s: s.activation(
                        sc_nat[:], ps1[:, 0:16], AF.Sigmoid), inc="ac")
                    p.wait("vector", "ac", ac_sg)
                    p.wait("vector", "pe", pe_fb)
                    p.vsem(lambda s: s.tensor_tensor(
                        sc_nat[:], sc_nat[:], psbc2[:, 0:16], ALU.add))
                    p.vsem(lambda s: s.tensor_copy(sc_cur[:], sc_nat[:]))
                    for _ in range(3):
                        p.vsem(lambda s: s.reduce_max(
                            sc_m[:], sc_cur[:], axis=AX.X))
                        p.vsem(lambda s: s.tensor_scalar(
                            sc_ge[:], sc_cur[:], sc_m[:], None, ALU.is_ge))
                        p.vsem(lambda s: s.scalar_tensor_tensor(
                            sc_cur[:], sc_ge[:], -1e30, sc_cur[:], ALU.mult, ALU.add))
                    p.vsem(lambda s: s.reduce_max(
                        sc_m[:], sc_cur[:], axis=AX.X))
                    p.vsem(lambda s: s.tensor_scalar(
                        sc_ge[:], sc_nat[:], sc_m[:], None, ALU.is_ge))
                    p.vsem(lambda s: s.tensor_tensor(
                        sc_ge[:], sc_ge[:], sc_nat[:], ALU.mult))
                    p.vsem(lambda s: s.reduce_sum(
                        sc_m[:], sc_ge[:], axis=AX.X))
                    p.vsem(lambda s: s.tensor_single_scalar(
                        sc_m[:], sc_m[:], 1e-8, ALU.max))
                    p.vsem(lambda s: s.reciprocal(sc_m[:], sc_m[:]))
                    gn_ve = p.add("vector", lambda s, tt=tt: s.tensor_scalar(
                        gall[:, 16 * tt:16 * (tt + 1)], sc_ge[:], sc_m[:], None,
                        ALU.mult), inc="ve")
                    p.wait("tensor", "ve", gn_ve)
                    p.wait("tensor", "ac", gt_ac)
                    pe_gt = p.add("tensor", lambda s, tt=tt: s.transpose(
                        pst_g[:, :], gall[:, 16 * tt:16 * (tt + 1)], identf[:]),
                        inc="pe")
                    p.wait("scalar", "pe", pe_gt)
                    p.wait("scalar", "gq", gq_b2)
                    gt_ac = p.add("scalar", lambda s, tt=tt: s.activation(
                        gt_aug[0:16, 128 * tt:128 * (tt + 1)], pst_g[:, :],
                        AF.Identity), inc="ac")

                w1_pe = [0, 0]
                w2_pe = 0
                ysl_ve = [0, 0]
                order = [16, 17] + list(range(16))
                for ei, e in enumerate(order):
                    shared = e >= 16
                    w1src = (sw1[e - 16] if shared else rw1[e])
                    w2src = (sw2[e - 16] if shared else rw2[e])
                    for ft in range(NFT):
                        slc = ft % 2
                        p.wait("gpsimd", "pe", w1_pe[slc])
                        w13 = w1src[:, 128 * ft:128 * (ft + 1)].rearrange(
                            "(k p) j -> p k j", p=128)
                        w1d = w1c[slc][:].rearrange("p (k j) -> p k j", k=16, j=128)
                        p.dma("gpsimd", "gq", w1d[:, 0:8, :], w13[:, 0:8, :])
                        gq_w1 = p.dma("gpsimd", "gq", w1d[:, 8:16, :], w13[:, 8:16, :])
                        p.wait("tensor", "gq", gq_w1)
                        for tc in range(2):
                            slot = tc
                            pdst = (psA, psB)[slot]
                            p.wait("tensor", "ac", free_ac[slot])
                            for k in range(16):
                                p.add("tensor",
                                      lambda s, pdst=pdst, k=k, tc=tc, slc=slc:
                                      s.matmul(pdst[:, 0:512],
                                               w1c[slc][:, 128 * k:128 * (k + 1)],
                                               xnT_m[:, k * 1024 + 512 * tc:k * 1024 + 512 * (tc + 1)],
                                               start=(k == 0), stop=(k == 15)),
                                      inc=("pe" if k == 15 else None))
                            pe_h = p.K["pe"]
                            if tc == 1:
                                w1_pe[slc] = pe_h
                            p.wait("scalar", "pe", pe_h)
                            ac_h = p.add("scalar",
                                         lambda s, pdst=pdst, ft=ft, tc=tc, e=e:
                                         s.activation(
                                             hT[:, ft * 1024 + 512 * tc:ft * 1024 + 512 * (tc + 1)],
                                             pdst[:, 0:512], AF.Relu,
                                             bias=b1_all[:, e * NFT + ft:e * NFT + ft + 1]),
                                         inc="ac")
                            free_ac[slot] = ac_h
                    ac_relu = p.K["ac"]
                    p.wait("gpsimd", "pe", w2_pe)
                    for ft in range(NFT):
                        p.dma("gpsimd", "gq", w2b[:, ft * 2048:(ft + 1) * 2048],
                              w2src[128 * ft:128 * (ft + 1), :])
                    gq_w2 = p.K["gq"]
                    p.wait("tensor", "ac", ac_relu)
                    p.wait("tensor", "gq", gq_w2)
                    for tt in range(8):
                        for dc in range(4):
                            slot = (4 * tt + dc) % 2
                            pdst = (ps1, ps2)[slot]
                            p.wait("tensor", "ve", ysl_ve[slot])
                            if ei == 0 and tt == 0 and dc == 0:
                                p.wait("tensor", "ac", gt_ac)
                            last_is_b2 = (e == 17)
                            for ft in range(NFT):
                                il = (ft == NFT - 1) and not last_is_b2
                                p.add("tensor",
                                      lambda s, pdst=pdst, ft=ft, tt=tt, dc=dc, il=il:
                                      s.matmul(pdst[:, :],
                                               hT[:, ft * 1024 + 128 * tt:ft * 1024 + 128 * (tt + 1)],
                                               w2b[:, ft * 2048 + 512 * dc:ft * 2048 + 512 * (dc + 1)],
                                               start=(ft == 0), stop=il),
                                      inc=("pe" if il else None))
                            if last_is_b2:
                                p.add("tensor", lambda s, pdst=pdst, tt=tt, dc=dc:
                                      s.matmul(pdst[:, :],
                                               gt_aug[0:18, 128 * tt:128 * (tt + 1)],
                                               b2aug[0:18, 512 * dc:512 * (dc + 1)],
                                               start=False, stop=True), inc="pe")
                            pe_y = p.K["pe"]
                            p.wait("vector", "pe", pe_y)
                            col = tt * DM + 512 * dc
                            if ei == 0:
                                ysl_ve[slot] = p.add(
                                    "vector", lambda s, pdst=pdst, col=col:
                                    s.tensor_copy(y_acc[:, col:col + 512], pdst[:, :]),
                                    inc="ve")
                            elif shared:
                                ysl_ve[slot] = p.add(
                                    "vector", lambda s, pdst=pdst, col=col:
                                    s.tensor_tensor(y_acc[:, col:col + 512],
                                                    y_acc[:, col:col + 512],
                                                    pdst[:, :], ALU.add), inc="ve")
                            else:
                                gcol = 16 * tt + e
                                ysl_ve[slot] = p.add(
                                    "vector",
                                    lambda s, pdst=pdst, col=col, gcol=gcol:
                                    s.scalar_tensor_tensor(
                                        y_acc[:, col:col + 512], pdst[:, :],
                                        gall[:, gcol:gcol + 1],
                                        y_acc[:, col:col + 512],
                                        ALU.mult, ALU.add), inc="ve")
                    w2_pe = p.K["pe"]
                    hT_pe = p.K["pe"]
                ve_y = p.K["ve"]
                exp_done = [("pe", p.K["pe"]), ("ve", ve_y), ("ac", p.K["ac"]),
                            ("gq", p.K["gq"])]

            with ExitStack() as es5:
                scr1b = es5.enter_context(nc.sbuf_tensor("scr1b", [128, DM], F32))
                xt5 = [es5.enter_context(nc.sbuf_tensor(f"xt5{i}", [128, DM], F32))
                       for i in range(2)]
                xg = es5.enter_context(nc.sbuf_tensor("xg", [128, DM], F32))
                gsb = [es5.enter_context(nc.sbuf_tensor(f"gsb{i}", [128, DM], BF16))
                       for i in range(2)]
                p.barrier_into(("sync", "scalar", "vector"), exp_done)
                slot_ac5 = [0, 0]
                gsb_sq = [0, 0]
                ve_stats5 = ve_y
                ve_gx = 0
                for tt in range(8):
                    sl5 = tt % 2
                    p.wait("sync", "ac", slot_ac5[sl5])
                    ld = p.dma("sync", "sq", xt5[sl5][:],
                               x_mine[128 * tt:128 * (tt + 1), :])
                    p.wait("scalar", "sq", ld)
                    p.wait("scalar", "ve", ve_gx)
                    ac_x, ve_stats5 = emit_ln_tile(xt5[sl5][:], xg[:], scr1b, ve_stats5)
                    slot_ac5[sl5] = ac_x
                    p.wait("vector", "ac", ac_x)
                    ve_gx = p.add("vector", lambda s, tt=tt: s.tensor_tensor(
                        xg[:], xg[:], y_acc[:, tt * DM:(tt + 1) * DM], ALU.add),
                        inc="ve")
                    p.wait("scalar", "ve", ve_gx)
                    p.wait("scalar", "sq", gsb_sq[sl5])
                    ac_gt5 = p.add("scalar", lambda s, sl5=sl5: s.activation(
                        gsb[sl5][:], xg[:], AF.Silu), inc="ac")
                    p.wait("sync", "ac", ac_gt5)
                    gsb_sq[sl5] = p.dma("sync", "sq",
                                        g_dram[128 * tt:128 * (tt + 1), :], gsb[sl5][:])
            ac_g = p.K["ac"]
            esY.close()
            ffn_done = [("pe", p.K["pe"]), ("ve", p.K["ve"]), ("ac", ac_g),
                        ("gq", p.K["gq"]), ("sq", p.K["sq"])]

            with ExitStack() as esC:
                nsb = lambda n, shp, d: esC.enter_context(nc.sbuf_tensor(n, shp, d))
                a_nat = nsb("a_nat", [128, 16 * DM], BF16)
                cbb = nsb("cbb", [128, 16 * 1024], BF16)
                an_sb = nsb("an_sb", [128, DM], BF16)
                scr6 = nsb("scr6", [128, 128], F32)

                p.barrier_into(("sync",), ffn_done)
                p.wait("sync", "cc", cc_gq)
                for k in range(16):
                    p.dma("sync", "sq", a_nat[:, k * DM:k * DM + 1024],
                          a_gath[128 * k:128 * (k + 1), :])
                    p.dma("sync", "sq", a_nat[:, k * DM + 1024:(k + 1) * DM],
                          a_gath[T + 128 * k:T + 128 * (k + 1), :])
                for k in range(16):
                    p.dma("sync", "sq", cbb[:, k * 1024:(k + 1) * 1024],
                          cmat[128 * k:128 * (k + 1), :])
                sq_conv = p.K["sq"]

                p.wait("tensor", "sq", sq_conv)
                p.barrier_into(("tensor",), [("ve", ffn_done[1][1]),
                                             ("ac", ac_g)])
                an_sq = 0
                ve_an = 0
                for tt in range(8):
                    p.wait("tensor", "ve", ve_an)
                    for dc in range(4):
                        pdst = (psA, psB)[dc // 2]
                        off = 512 * (dc % 2)
                        for k in range(16):
                            p.add("tensor",
                                  lambda s, pdst=pdst, off=off, k=k, tt=tt, dc=dc:
                                  s.matmul(pdst[:, off:off + 512],
                                           cbb[:, k * 1024 + 128 * tt:k * 1024 + 128 * (tt + 1)],
                                           a_nat[:, k * DM + 512 * dc:k * DM + 512 * (dc + 1)],
                                           start=(k == 0), stop=(k == 15)),
                                  inc=("pe" if k == 15 else None))
                    pe_c = p.K["pe"]
                    p.wait("scalar", "pe", pe_c)
                    p.wait("scalar", "ve", ve_an)
                    for hloc in range(16):
                        pdst = (psA, psB)[hloc // 8]
                        off = 128 * (hloc % 8)
                        bld = (lambda s, pdst=pdst, off=off, hloc=hloc:
                               s.activation(scr6[:], pdst[:, off:off + 128],
                                            AF.Square,
                                            accum_out=ms16[:, hloc:hloc + 1]))
                        if hloc == 15:
                            p.asem(bld)
                        else:
                            p.add("scalar", bld)
                    p.asem(lambda s: s.activation(
                        rstd16[:], ms16[:], AF.Ln, bias=np.float32(EPS), scale=1.0 / 128))
                    ac_rs = p.add("scalar", lambda s: s.activation(
                        rstd16[:], rstd16[:], AF.Exp, scale=-0.5), inc="ac")
                    p.wait("vector", "ac", ac_rs)
                    p.wait("vector", "sq", an_sq)
                    for hloc in range(16):
                        pdst = (psA, psB)[hloc // 8]
                        off = 128 * (hloc % 8)
                        p.add("vector", lambda s, pdst=pdst, off=off, hloc=hloc:
                              s.tensor_scalar(an_sb[:, 128 * hloc:128 * (hloc + 1)],
                                              pdst[:, off:off + 128],
                                              rstd16[:, hloc:hloc + 1], None, ALU.mult),
                              inc=("ve" if hloc == 15 else None))
                    ve_an = p.K["ve"]
                    p.wait("sync", "ve", ve_an)
                    an_sq = p.dma("sync", "sq",
                                  an_dram[128 * tt:128 * (tt + 1), :], an_sb[:])
                conv_done = [("pe", p.K["pe"]), ("ve", p.K["ve"]), ("sq", p.K["sq"])]

            with ExitStack() as esW:
                nsb = lambda n, shp, d: esW.enter_context(nc.sbuf_tensor(n, shp, d))
                woutb = nsb("woutb", [128, 16 * DM], BF16)
                wtmp = [nsb(f"wtmp{i}", [128, DM], F32) for i in range(2)]
                anT = nsb("anT", [128, 16 * 1024], BF16)
                xt7 = [nsb(f"xt7{i}", [128, DM], F32) for i in range(2)]
                outb = [nsb(f"outb{i}", [128, DM], F32) for i in range(2)]
                gld = [nsb(f"gld{i}", [128, DM], BF16) for i in range(2)]
                t7 = nsb("t7", [128, 512], F32)

                p.barrier_into(("sync",), conv_done)
                for k in range(16):
                    p.dmat("sync", "sq", anT[:, k * 1024:(k + 1) * 1024],
                           an_dram[:, 128 * k:128 * (k + 1)])
                sq_anT = p.K["sq"]
                wslot_ve = [0, 0]
                for k in range(16):
                    sl7 = k % 2
                    p.wait("sync", "ve", wslot_ve[sl7])
                    ld = p.dma("sync", "sq", wtmp[sl7][:],
                               wout.ap()[128 * k:128 * (k + 1), :])
                    p.wait("vector", "sq", ld)
                    p.wait("vector", "ve", conv_done[1][1] if k == 0 else 0)
                    wslot_ve[sl7] = p.add("vector",
                                          lambda s, sl7=sl7, k=k: s.tensor_scalar(
                                              woutb[:, k * DM:(k + 1) * DM],
                                              wtmp[sl7][:], rmsw_sc[:], None, ALU.mult),
                                          inc="ve")
                ve_wout = p.K["ve"]

                p.wait("tensor", "sq", sq_anT)
                p.wait("tensor", "ve", ve_wout)
                xslot_ve = [0, 0]
                outb_so = [0, 0]
                psl_ve = [0, 0]
                for tt in range(8):
                    sl7 = tt % 2
                    p.wait("sync", "ve", xslot_ve[sl7])
                    ldx = p.dma("sync", "sq", xt7[sl7][:],
                                x_mine[128 * tt:128 * (tt + 1), :])
                    ldg = p.dma("sync", "sq", gld[sl7][:],
                                g_dram[128 * tt:128 * (tt + 1), :])
                    for dc in range(4):
                        slot = dc % 2
                        pdst = (ps1, ps2)[slot]
                        p.wait("tensor", "ve", psl_ve[slot])
                        for k in range(16):
                            p.add("tensor", lambda s, pdst=pdst, k=k, tt=tt, dc=dc:
                                  s.matmul(pdst[:, :],
                                           anT[:, k * 1024 + 128 * tt:k * 1024 + 128 * (tt + 1)],
                                           woutb[:, k * DM + 512 * dc:k * DM + 512 * (dc + 1)],
                                           start=(k == 0), stop=(k == 15)),
                                  inc=("pe" if k == 15 else None))
                        pe_s7 = p.K["pe"]
                        p.wait("vector", "pe", pe_s7)
                        p.wait("vector", "sq", ldg)
                        p.wait("vector", "so", outb_so[sl7] if dc == 0 else 0)
                        p.vsem(lambda s, pdst=pdst, sl7=sl7, dc=dc:
                               s.tensor_tensor(
                                   t7[:], pdst[:, :],
                                   gld[sl7][:, 512 * dc:512 * (dc + 1)],
                                   ALU.mult))
                        psl_ve[slot] = p.add(
                            "vector", lambda s, sl7=sl7, dc=dc: s.tensor_tensor(
                                outb[sl7][:, 512 * dc:512 * (dc + 1)], t7[:],
                                xt7[sl7][:, 512 * dc:512 * (dc + 1)], ALU.add),
                            inc="ve")
                    xslot_ve[sl7] = max(psl_ve)
                    p.wait("sync", "ve", p.K["ve"])
                    outb_so[sl7] = p.dma("sync", "so",
                                         out_d[128 * tt:128 * (tt + 1), :],
                                         outb[sl7][:])
                p.wait("sync", "so", p.K["so"])

        sems = {}
        with ExitStack() as es_sem:
            for name in ("sq", "gq", "pe", "ac", "ve", "so", "vv", "aa", "cc"):
                sems[name] = es_sem.enter_context(nc.semaphore(name))
            with nc.Block() as block:
                p.emit(block, sems)
    return nc


_CACHE = {}


def _prep_inputs(inputs):
    C, CC, SS, perm, masks4, ident = host_constants()
    x = np.ascontiguousarray(np.asarray(inputs["L0_attn_query"], np.float32))
    wq_ = np.ascontiguousarray(np.asarray(inputs["L0_attn_Wq"], np.float32)[:, perm])
    wk_ = np.ascontiguousarray(np.asarray(inputs["L0_attn_Wk"], np.float32)[:, perm])
    sb1 = np.asarray(inputs["L0_ffn_shared_b1"], np.float32)
    rb1 = np.asarray(inputs["L0_ffn_routed_b1"], np.float32)
    wv_ = np.asarray(inputs["L0_attn_Wv"], np.float32)
    shared = dict(
        wout=np.ascontiguousarray(np.asarray(inputs["L0_attn_Wout"], np.float32)),
        rmsw=np.asarray(inputs["L0_attn_rms_weight"], np.float32).reshape(128, 1),
        lam_in=np.concatenate(
            [np.asarray(inputs["L0_attn_lambda_q1"], np.float32),
             np.asarray(inputs["L0_attn_lambda_k1"], np.float32),
             np.asarray(inputs["L0_attn_lambda_q2"], np.float32),
             np.asarray(inputs["L0_attn_lambda_k2"], np.float32)]).reshape(1, 256),
        sw1=np.ascontiguousarray(np.asarray(inputs["L0_ffn_shared_w1"], np.float32)),
        sb1r=np.ascontiguousarray(sb1.reshape(2, NFT, 128).transpose(0, 2, 1)),
        sw2=np.ascontiguousarray(np.asarray(inputs["L0_ffn_shared_w2"], np.float32)),
        rw1=np.ascontiguousarray(np.asarray(inputs["L0_ffn_routed_w1"], np.float32)),
        rb1r=np.ascontiguousarray(rb1.reshape(16, NFT, 128).transpose(0, 2, 1)),
        rw2=np.ascontiguousarray(np.asarray(inputs["L0_ffn_routed_w2"], np.float32)),
        b2r=np.ascontiguousarray(np.asarray(inputs["L0_ffn_routed_b2"], np.float32)),
        b2s=np.ascontiguousarray(np.asarray(inputs["L0_ffn_shared_b2"], np.float32)),
        centT=np.ascontiguousarray(
            np.asarray(inputs["L0_ffn_centroids"], np.float32).T),
        fbias=np.asarray(inputs["L0_ffn_bias"], np.float32).reshape(1, 16),
        cosT_in=CC, sinT_in=SS, masks_in=masks4, ident_in=ident,
        identf_in=ident.astype(np.float32),
    )
    in_maps = []
    for c in range(8):
        b, half = c // 2, c % 2
        t0 = 1024 * half
        m = dict(shared)
        m["x_full"] = np.ascontiguousarray(x[b])
        m["x_mine"] = np.ascontiguousarray(x[b, t0:t0 + 1024])
        hc = 1024 * (c % 2)
        m["wq"] = np.ascontiguousarray(wq_[:, hc:hc + 1024])
        m["wk"] = np.ascontiguousarray(wk_[:, hc:hc + 1024])
        m["wv"] = np.ascontiguousarray(wv_[:, hc:hc + 1024])
        m["cmat"] = np.ascontiguousarray(C[t0:t0 + 1024, :].T).astype(BF)
        in_maps.append(m)
    return in_maps


def kernel(**inputs):
    if "nc" not in _CACHE:
        _CACHE["nc"] = build_program()
    nc = _CACHE["nc"]
    in_maps = _prep_inputs(inputs)
    res = run_bass_kernel_spmd(nc, in_maps, core_ids=list(range(8)))
    out = np.empty((4, T, DM), np.float32)
    for c in range(8):
        b, half = c // 2, c % 2
        out[b, 1024 * half:1024 * half + 1024] = np.asarray(
            res.results[c]["out"], np.float32)
    return out
